# version 32 (final)
# baseline (speedup 1.0000x reference)
import contextlib
import numpy as np
import concourse.bass as bass
import concourse.mybir as mybir
from concourse.bass_utils import run_bass_kernel_spmd

F32 = mybir.dt.float32
BF16 = mybir.dt.bfloat16
I32 = mybir.dt.int32
I16 = mybir.dt.int16
AF = mybir.ActivationFunctionType
ALU = mybir.AluOpType
AX = mybir.AxisListType

P, CC, HC, TS, NSEG = 128, 8, 32, 512, 4
T, C, H = TS * NSEG, P * CC, P * HC


def build_program():
    nc = bass.Bass("TRN2", target_bir_lowering=False, debug=False)

    xin = nc.dram_tensor("xin", [P, CC, T], F32, kind="ExternalInput").ap()
    wts = {n: nc.dram_tensor(n, [P, CC, C], F32, kind="ExternalInput").ap()
           for n in ("wkt", "wvt", "wgt", "w0n", "wqt", "wot")}
    w1t = nc.dram_tensor("w1t", [P, CC, H], F32, kind="ExternalInput").ap()
    w2t = nc.dram_tensor("w2t", [P, HC, C], F32, kind="ExternalInput").ap()
    a_t = nc.dram_tensor("a_t", [8, C], F32, kind="ExternalInput").ap()
    b_i = nc.dram_tensor("b_i", [8, C], F32, kind="ExternalInput").ap()
    pv = {n: nc.dram_tensor(n, [P, CC], F32, kind="ExternalInput").ap()
          for n in ("vlr", "vg", "vb", "vls")}
    vcw = nc.dram_tensor("vcw", [P, HC, 3], F32, kind="ExternalInput").ap()
    vcb = nc.dram_tensor("vcb", [P, HC], F32, kind="ExternalInput").ap()
    out = nc.dram_tensor("out", [P, CC, T], F32, kind="ExternalOutput").ap()

    at_s = nc.dram_tensor("at_s", [6, P, CC, C], BF16, kind="Internal").ap()
    w1_s = nc.dram_tensor("w1_s", [8, P, CC, TS], BF16, kind="Internal").ap()
    w2_s = nc.dram_tensor("w2_s", [CC, P, HC, P], BF16, kind="Internal").ap()
    x2_s = nc.dram_tensor("x2_s", [P, CC, T], F32, kind="Internal").ap()
    xq_s = nc.dram_tensor("xq_s", [P, CC, T], BF16, kind="Internal").ap()

    es = contextlib.ExitStack()
    sb = lambda name, shape, dt: es.enter_context(nc.sbuf_tensor(name, shape, dt))

    fu = [sb(f"fu{i}", [P, CC, 513], F32) for i in range(7)]
    bu = [sb(f"bu{i}", [P, CC, TS], BF16) for i in range(9)]
    pos = sb("pos", [P, T], BF16)
    qiw2 = sb("qiw2", [P, 2, TS], I16)
    qcw2 = sb("qcw2", [P, 2, TS], BF16)
    ones5 = sb("ones5", [P, TS], F32)
    idf = sb("idf", [P, P], F32)
    ones_c = sb("ones_c", [P, 1], BF16)
    ones_cf = sb("ones_cf", [P, 1], F32)
    ones_rf = sb("ones_rf", [1, P], F32)
    nh1 = sb("nh1", [1, 1], F32)
    ph1 = sb("ph1", [1, 1], F32)
    lrv = sb("lrv", [P, CC], F32)
    gvv = sb("gvv", [P, CC], F32)
    bvv = sb("bvv", [P, CC], F32)
    lsv = sb("lsv", [P, CC], F32)
    cwv = sb("cwv", [P, HC * 3], F32)
    cbv = sb("cbv", [P, HC], F32)
    m2v = sb("m2v", [P, 1], F32)
    m1v = sb("m1v", [P, 1], F32)
    sw1v = sb("sw1v", [P, 1], F32)
    sw2v = sb("sw2v", [P, 1], F32)
    wacc = sb("wacc", [P, 48], F32)
    wrow = sb("wrow", [1, 8], F32)
    reps = [sb(f"rep{i}", [P, TS], F32) for i in range(3)]
    repb = sb("repb", [P, TS], BF16)
    carry = sb("carry", [P, CC], F32)
    qiw = sb("qiw", [P, 2, TS], I16)
    qcw = sb("qcw", [P, 2, TS], BF16)
    qia = qiw[:, 0]
    qib = qiw[:, 1]
    qcb = qcw[:, 0]
    qzb = qcw[:, 1]
    smx = sb("smx", [P, 4], F32)
    tails = sb("tails", [P, HC, 2], F32)
    stail = sb("stail", [P, HC], F32)
    qtail = sb("qtail", [P, HC], BF16)
    ttmp = sb("ttmp", [P, CC], F32)

    pb = [es.enter_context(nc.psum_tensor(f"pb{i}", [P, TS], F32))
          for i in range(8)]

    SEMS = ("ldp", "ldx0", "ldx1", "lda0", "lda1", "ldq0", "ldq1",
            "ldfx", "ldf10", "ldf11", "ldf20", "ldf21", "spl", "sout",
            "sp0", "sp1", "ldq2", "ldq3", "spy",
            "x2w", "mrw", "t", "v", "s", "g", "wfa", "wff")
    sems, cnt = {}, {}
    for n in SEMS:
        sems[n] = es.enter_context(nc.semaphore("sem_" + n))
        cnt[n] = 0

    def bump(name, instr, step=1):
        instr.then_inc(sems[name], step)
        cnt[name] += step
        return cnt[name]

    dinc = lambda name, instr: bump(name, instr, 16)
    einc = lambda name, instr: bump(name, instr, 1)

    X = [fu[0], fu[1]]
    H1, KF, VS, Y0, WQS = fu[2], fu[3], fu[4], fu[5], fu[6]
    HB, KB, OB, GB, XQ, XS = bu[0], bu[1], bu[2], bu[3], bu[4], bu[5]
    WS = [bu[6], bu[7]]
    WQB = bu[8]
    SILU = [bu[0], bu[1], bu[2], bu[3]]
    W1U = [bu[6], bu[7]]
    W2U = [bu[5], bu[8]]
    X2W, OUTS = fu[3], fu[4]

    def flat(u):
        return u[:, :, :].rearrange("p a b -> p (a b)")

    def fl512(u):
        return u[:, :, 0:TS]

    def w2view(u):
        return flat(u).rearrange("p (a b) -> p a b", b=P)

    wqsf = WQS[:, :, :].rearrange("p a b -> p (a b)")
    cva = [wqsf[:, 0:TS], wqsf[:, TS:2 * TS]]
    cvb = [wqsf[:, 2 * TS:3 * TS], wqsf[:, 3 * TS:4 * TS]]
    hmt = [wqsf[:, 4 * TS:5 * TS], wqsf[:, 5 * TS:6 * TS]]
    rmx = wqsf[:, 6 * TS:7 * TS]
    reps.append(wqsf[:, 7 * TS:8 * TS])
    qc2 = wqsf[:, 3 * TS:4 * TS]
    qd2 = wqsf[:, TS:3 * TS]
    mrow = reps[2][0:4, 0:P]
    gtail = wqsf[:, 8 * TS:8 * TS + CC]
    def bc(r, n=TS):
        return r[:, None, 0:n].broadcast_to([P, CC, n])

    def silu_ap(hc, n=TS):
        return SILU[hc // 8][:, hc % 8, 0:n]

    blocks = []

    def block():
        b = nc.Block()
        blocks.append(b)
        return b

    with block() as blk:
        @blk.sync
        def _(sy):
            for src, dst in ((pv["vlr"], lrv), (pv["vg"], gvv), (pv["vb"], bvv),
                             (pv["vls"], lsv), (vcb, cbv)):
                dinc("ldp", sy.dma_start(out=dst[:], in_=src))
            dinc("ldp", sy.dma_start(
                out=cwv[:].rearrange("p (a b) -> p a b", b=3), in_=vcw))
            dinc("ldp", sy.dma_start(out=flat(WQS)[0:8, 0:C], in_=a_t))
            dinc("ldp", sy.dma_start(out=flat(WQS)[0:8, C:2 * C], in_=b_i))
            dinc("ldx0", sy.dma_start(out=fl512(X[0]), in_=xin[:, :, 0:TS]))
        ldp_all = cnt["ldp"]

        @blk.gpsimd
        def _(g):
            g.memset(ones5[:], 1.0)
            g.memset(ones_c[:], 1.0)
            g.memset(ones_cf[:], 1.0)
            g.memset(ones_rf[:], 1.0)
            g.memset(nh1[:], -0.5)
            g.memset(ph1[:], 0.5)
            g.memset(carry[:], 0.0)
            g.memset(gtail[:], 0.0)
            g.memset(tails[:, :, :].rearrange("p a b -> p (a b)"), 0.0)
            g.iota(idf[:], [[1, P]], base=0, channel_multiplier=-1,
                   allow_small_or_imprecise_dtypes=True)
            g.drain()
            g.tensor_scalar(out=idf[:], in0=idf[:], scalar1=0.0, scalar2=None,
                            op0=ALU.is_equal)
            g.drain()
            einc("g", g.iota(wqsf[0:1, 2 * C:2 * C + T], [[1, T]], base=1,
                             channel_multiplier=0,
                             allow_small_or_imprecise_dtypes=True))
        g_cst = cnt["g"]

        @blk.scalar
        def _(sc):
            sc.wait_ge(sems["g"], g_cst)
            sc.activation(out=wqsf[0:1, 2 * C:2 * C + T],
                          in_=wqsf[0:1, 2 * C:2 * C + T], func=AF.Ln)
            sc.drain()
            sc.wait_ge(sems["ldp"], ldp_all)
            einc("s", sc.activation(out=lrv[:], in_=lrv[:], func=AF.Exp))
            sc.drain()
        s_p0 = cnt["s"]

        @blk.vector
        def _(v):
            v.wait_ge(sems["s"], s_p0)
            v.tensor_scalar(out=lrv[:], in0=lrv[:], scalar1=1.0, scalar2=None,
                            op0=ALU.min)
            v.drain()
            v.tensor_scalar(out=wqsf[0:1, 2 * C:2 * C + T],
                            in0=wqsf[0:1, 2 * C:2 * C + T],
                            scalar1=0.1, scalar2=1.0, op0=ALU.mult, op1=ALU.add)
            v.drain()
            einc("v", v.reciprocal(out=wqsf[0:1, 2 * C:2 * C + T],
                                   in_=wqsf[0:1, 2 * C:2 * C + T]))
            v.drain()
        v_p0 = cnt["v"]

        t_pb = cnt["t"]
        s_pb = cnt["s"]

        @blk.tensor
        def _(t):
            t.wait_ge(sems["v"], v_p0)
            for j in range(NSEG):
                if j >= 2:
                    t.wait_ge(sems["s"], s_pb + j - 1)
                einc("t", t.matmul(pb[2 + j % 2][:], ones_rf[:],
                                   wqsf[0:1, 2 * C + j * TS:2 * C + (j + 1) * TS],
                                   start=True, stop=True))

        @blk.scalar
        def _(sc):
            for j in range(NSEG):
                sc.wait_ge(sems["t"], t_pb + j + 1)
                einc("s", sc.copy(out=pos[:, j * TS:(j + 1) * TS],
                                  in_=pb[2 + j % 2][:]))
                sc.drain()
    mark_lr = v_p0
    mark_posmm = cnt["t"]

    aorder = ["wkt", "wvt", "wgt", "w0n", "wqt", "wot"]
    with block() as blk:
        @blk.vector
        def _(v):
            v.wait_ge(sems["ldp"], ldp_all)
            v.tensor_copy(out=flat(WQB)[0:8, 0:C], in_=flat(WQS)[0:8, 0:C])
            v.drain()
            einc("v", v.tensor_copy(out=flat(WQB)[0:8, C:2 * C],
                                    in_=flat(WQS)[0:8, C:2 * C]))
            v.drain()
        v_ab = cnt["v"]

        lda_b = [cnt["lda0"], cnt["lda1"]]
        v_b = cnt["v"]
        t_b = cnt["t"]
        spl_b = cnt["spl"]
        nslab = 6 * CC

        @blk.sync
        def _(sy):
            sy.wait_ge(sems["v"], v_ab)
            sy.wait_ge(sems["t"], mark_posmm)
            for i in range(nslab):
                wi, ccx = divmod(i, CC)
                if i >= 2:
                    sy.wait_ge(sems["v"], v_b + i - 1)
                buf = 2 + i % 2
                dinc("lda%d" % (i % 2), sy.dma_start(
                    out=flat(WQS)[:, buf * C:(buf + 1) * C],
                    in_=wts[aorder[wi]][:, ccx, :]))

        @blk.tensor
        def _(t):
            t.wait_ge(sems["v"], v_ab)
            for k in range(16):
                ccx, hf = divmod(k, 2)
                if k >= 2:
                    t.wait_ge(sems["v"], v_b + 24 + (k - 2) // 2 + 1)
                einc("t", t.matmul(pb[4 + k % 2][:],
                                   flat(WQB)[0:8, ccx * P:(ccx + 1) * P],
                                   flat(WQB)[0:8, C + hf * TS:C + (hf + 1) * TS],
                                   start=True, stop=True))

        @blk.vector
        def _(v):
            for i in range(nslab):
                wi, ccx = divmod(i, CC)
                buf = 2 + i % 2
                tcol = buf * C
                v.wait_ge(sems["lda%d" % (i % 2)],
                          lda_b[i % 2] + 16 * (i // 2 + 1))
                if i >= 2:
                    v.wait_ge(sems["ldfx" if i % 2 == 0 else "spy"],
                              16 * (i // 2))
                if wi == 3:
                    for hf in range(2):
                        k = ccx * 2 + hf
                        v.wait_ge(sems["t"], t_b + k + 1)
                        ii = v.tensor_tensor(
                            out=flat(WQB)[:, tcol + hf * TS:
                                          tcol + (hf + 1) * TS],
                            in0=flat(WQS)[:, tcol + hf * TS:
                                          tcol + (hf + 1) * TS],
                            in1=pb[4 + k % 2][:], op=ALU.add)
                        if hf == 0:
                            v.drain()
                    einc("v", ii)
                else:
                    einc("v", v.tensor_copy(
                        out=flat(WQB)[:, tcol:tcol + C],
                        in_=flat(WQS)[:, tcol:tcol + C]))

        @blk.scalar
        def _(sc):
            for i in range(nslab):
                wi, ccx = divmod(i, CC)
                tcol = (2 + i % 2) * C
                sc.wait_ge(sems["v"], v_b + i + 1)
                dinc("ldfx" if i % 2 == 0 else "spy",
                     sc.dma_start(out=at_s[wi, :, ccx, :],
                                  in_=flat(WQB)[:, tcol:tcol + C]))
    p1_sp0 = cnt["ldfx"]
    p1_sp1 = cnt["spy"]

    x1f = flat(X[1])
    slab = [x1f[:, i * 1024:(i + 1) * 1024] for i in range(4)]
    wqbf = flat(WQB)
    terL = [wqbf[:, 0:1024], wqbf[:, 2048:3072]]
    terG = [wqbf[:, 1024:2048], wqbf[:, 3072:4096]]
    pm = {}
    pst = {"am": {}, "vm": {}, "ldw": {}, "spw": {},
           "ld": [0, 0, 0, 0], "sp": [0, 0], "next_ld": 0}

    def sem_idx(c):
        if 32 <= c < 48:
            return 2 * ((c - 32) % 2)
        return c % 4

    def chunk_src(c):
        if c < 32:
            q, ccx = divmod(c, 8)
            return w1t[:, ccx, q * 1024:(q + 1) * 1024]
        if c < 48:
            k = c - 32
            p = k % 2
            return (x1f[:, 2048 * p:2048 * p + 2048]
                    .rearrange("p (a b) -> p a b", b=C),
                    w2t[:, 2 * k:2 * k + 2, :])
        j = c - 48
        if j < 32:
            q, ccx = divmod(j, 8)
            return w1t[:, ccx, q * 1024:(q + 1) * 1024]
        return w2t[:, j - 32, :]

    def ld_cons(c):
        if c < 4:
            return None
        if c < 32:
            return ("s", pst["am"][c - 4])
        if c < 48:
            if c < 34:
                return ("s", pst["am"][29 + 2 * (c - 32)])
            return ("s", pst["am"][c - 2])
        j = c - 48
        if j < 4:
            return ("s", pst["am"][46 + j // 2])
        return ("v", pst["vm"][c - 4])

    def emit_loads(blk, upto):
        upto = min(upto, 112)
        if pst["next_ld"] >= upto:
            return

        @blk.gpsimd
        def _(g):
            while pst["next_ld"] < upto:
                c = pst["next_ld"]
                try:
                    w = ld_cons(c)
                except KeyError:
                    break
                if w is not None:
                    g.wait_ge(sems[w[0]], w[1])
                else:
                    g.wait_ge(sems["ldp"], ldp_all)
                si = sem_idx(c)
                if 32 <= c < 48:
                    dst, ssrc = chunk_src(c)
                    dinc("ldq%d" % si, g.dma_start(out=dst, in_=ssrc))
                else:
                    dinc("ldq%d" % si, g.dma_start(
                        out=slab[c % 4], in_=chunk_src(c)))
                pst["ld"][si] += 1
                pst["ldw"][c] = 16 * pst["ld"][si]
                pst["next_ld"] = c + 1

    def emit_spills(blk, cs):
        @blk.gpsimd
        def _(g):
            for c in cs:
                g.wait_ge(sems["v"], pst["vm"][c])
                j = c - 48
                if j < 32:
                    q, ccx = divmod(j, 8)
                    ii = g.dma_start(
                        out=w1_s[2 * q:2 * q + 2, :, ccx, :].rearrange(
                            "g p t -> p g t"),
                        in_=terG[c % 2].rearrange("p (g t) -> p g t", g=2))
                else:
                    ii = g.dma_start(
                        out=w2_s[:, :, j - 32, :].rearrange("d p c -> p d c"),
                        in_=terG[c % 2].rearrange("p (d c) -> p d c", d=CC))
                dinc("sp%d" % (c % 2), ii)
                pst["sp"][c % 2] += 1
                pst["spw"][c] = 16 * pst["sp"][c % 2]

    def mk_means_batch(c0, nb):
        def emit(blk):
            emit_loads(blk, c0 + nb)

            @blk.scalar
            def _(sc):
                for c in range(c0, c0 + nb):
                    sc.wait_ge(sems["ldq%d" % sem_idx(c)], pst["ldw"][c])
                    if c < 32:
                        ap, col = slab[c % 4], c
                    else:
                        k = c - 32
                        p = k % 2
                        ap, col = x1f[:, 2048 * p:2048 * p + 2048], 32 + k
                    einc("s", sc.activation(out=ap, in_=ap, func=AF.Abs,
                                            accum_out=wacc[:, col:col + 1]))
                    pst["am"][c] = cnt["s"]
            emit_loads(blk, c0 + 2 * nb)
        return emit

    def mk_sw_bundle():
        def emit(blk):
            @blk.tensor
            def _(t):
                t.wait_ge(sems["s"], pst["am"][47])
                einc("t", t.matmul(pb[6][0:1, 0:48], ones_cf[:], wacc[:],
                                   start=True, stop=True))
            t_ws = cnt["t"]

            @blk.vector
            def _(v):
                v.wait_ge(sems["t"], t_ws)
                v.reduce_sum(out=wrow[0:1, 0:1], in_=pb[6][0:1, 0:32],
                             axis=AX.X)
                v.reduce_sum(out=wrow[0:1, 1:2], in_=pb[6][0:1, 32:48],
                             axis=AX.X)
                v.drain()
                v.tensor_scalar(out=wrow[0:1, 2:4], in0=wrow[0:1, 0:2],
                                scalar1=1.0 / (C * H), scalar2=1e-5,
                                op0=ALU.mult, op1=ALU.max)
                v.drain()
                einc("v", v.reciprocal(out=wrow[0:1, 4:6], in_=wrow[0:1, 2:4]))
                v.drain()
            v_sw = cnt["v"]

            @blk.tensor
            def _(t):
                t.wait_ge(sems["v"], v_sw)
                einc("t", t.matmul(pb[6][:, 0:4], ones_rf[:], wrow[0:1, 2:6],
                                   start=True, stop=True))
            t_swr = cnt["t"]

            @blk.scalar
            def _(sc):
                sc.wait_ge(sems["t"], t_swr)
                sc.copy(out=m1v[:], in_=pb[6][:, 0:1])
                sc.copy(out=m2v[:], in_=pb[6][:, 1:2])
                sc.copy(out=sw1v[:], in_=pb[6][:, 2:3])
                sc.drain()
                einc("s", sc.copy(out=sw2v[:], in_=pb[6][:, 3:4]))
                sc.drain()
            pm["s_sw"] = cnt["s"]

            @blk.vector
            def _(v):
                v.wait_ge(sems["s"], pm["s_sw"])
                einc("v", v.tensor_scalar(out=cwv[:], in0=cwv[:],
                                          scalar1=m1v[:], scalar2=None,
                                          op0=ALU.mult))
                v.drain()
            pm["v_cwv"] = cnt["v"]
        return emit

    def mk_tern_batch(c0):
        def emit(blk):
            emit_loads(blk, c0 + 4)
            for half in range(2):
                @blk.vector
                def _(v, half=half):
                    if c0 == 48 and half == 0:
                        v.wait_ge(sems["s"], pm["s_sw"])
                    for c in (c0 + 2 * half, c0 + 2 * half + 1):
                        sw = sw1v if (c - 48) < 32 else sw2v
                        v.wait_ge(sems["ldq%d" % sem_idx(c)], pst["ldw"][c])
                        if c >= 50:
                            v.wait_ge(sems["sp%d" % (c % 2)],
                                      pst["spw"][c - 2])
                        v.tensor_scalar(out=terL[c % 2], in0=slab[c % 4],
                                        scalar1=sw[:], scalar2=-0.5,
                                        op0=ALU.mult, op1=ALU.is_lt)
                        v.tensor_scalar(out=terG[c % 2], in0=slab[c % 4],
                                        scalar1=sw[:], scalar2=0.5,
                                        op0=ALU.mult, op1=ALU.is_gt)
                    v.drain()
                    for c in (c0 + 2 * half, c0 + 2 * half + 1):
                        einc("v", v.tensor_tensor(out=terG[c % 2],
                                                  in0=terG[c % 2],
                                                  in1=terL[c % 2],
                                                  op=ALU.subtract))
                        pst["vm"][c] = cnt["v"]
                    v.drain()
                emit_spills(blk, (c0 + 2 * half, c0 + 2 * half + 1))
            emit_loads(blk, c0 + 8)
        return emit

    prep_work = ([mk_means_batch(c, 4) for c in range(0, 32, 4)]
                 + [mk_means_batch(c, 2) for c in range(32, 48, 2)]
                 + [mk_sw_bundle()]
                 + [mk_tern_batch(c) for c in range(48, 112, 4)])

    pm["sp0_w1"] = 16 * len([c for c in range(48, 80) if c % 2 == 0])
    pm["sp1_w1"] = 16 * len([c for c in range(48, 80) if c % 2 == 1])
    pm["sp0_all"] = 16 * len([c for c in range(48, 112) if c % 2 == 0])
    pm["sp1_all"] = 16 * len([c for c in range(48, 112) if c % 2 == 1])

    def attn_front(blk, s, ctx, fill=None):
        if True:
            xs = X[s % 2]
            XF = XQ
            pa, pc = (pb[2], pb[3]) if s == 0 else (pb[6], pb[7])
            R1a = wqsf[32:33, 0:TS]
            R1b = wqsf[32:33, TS:2 * TS]
            R1c = wqsf[32:33, 2 * TS:3 * TS]
            ST = lambda k: wqsf[0:1, k * TS:(k + 1) * TS]

            @blk.vector
            def _(v):
                v.wait_ge(sems["ldx%d" % (s % 2)], 16 * (s // 2 + 1))
                if ctx is not None:
                    v.wait_ge(sems["t"], ctx["t_f1e"])
                einc("v", v.tensor_copy(out=XF[:, :, :], in_=fl512(xs)))
            v_xbf = cnt["v"]
            if fill is not None:
                fill(2)

            @blk.tensor
            def _(t):
                t.wait_ge(sems["v"], v_xbf)
                if ctx is not None:
                    t.wait_ge(sems["s"], ctx["egm7"])
                for ccx in range(CC):
                    ii = t.matmul(pa[0:1, :], ones_c[:], XF[:, ccx, :],
                                  start=(ccx == 0), stop=(ccx == CC - 1))
                einc("t", ii)
            t_sA = cnt["t"]
            if fill is not None:
                fill(2)

            @blk.scalar
            def _(sc):
                sc.wait_ge(sems["t"], t_sA)
                einc("s", sc.activation(out=XF[:, :, :], in_=fl512(xs),
                                        func=AF.Square))
            s_sq = cnt["s"]

            @blk.tensor
            def _(t):
                t.wait_ge(sems["s"], s_sq)
                for ccx in range(CC):
                    ii = t.matmul(pc[0:1, :], ones_c[:], XF[:, ccx, :],
                                  start=(ccx == 0), stop=(ccx == CC - 1))
                einc("t", ii)
            t_sB = cnt["t"]
            if fill is not None:
                fill(2)

            @blk.vector
            def _(v):
                v.wait_ge(sems["t"], t_sB)
                v.tensor_scalar(out=ST(0), in0=pa[0:1, :], scalar1=1.0 / C,
                                scalar2=None, op0=ALU.mult)
                v.tensor_scalar(out=R1a, in0=pc[0:1, :], scalar1=1.0 / C,
                                scalar2=None, op0=ALU.mult)
                v.drain()
                v.tensor_tensor(out=R1b, in0=ST(0), in1=ST(0), op=ALU.mult)
                v.drain()
                v.tensor_tensor(out=R1a, in0=R1a, in1=R1b, op=ALU.subtract)
                v.drain()
                einc("v", v.tensor_scalar(out=R1b, in0=R1a, scalar1=1e-5,
                                          scalar2=None, op0=ALU.add))
                v.drain()
            v_u1 = cnt["v"]
            if fill is not None:
                fill(2)

            @blk.scalar
            def _(sc):
                sc.wait_ge(sems["v"], v_u1)
                sc.activation(out=ST(3), in_=R1b, func=AF.Ln)
                sc.drain()
                einc("s", sc.activation(out=ST(1), in_=ST(3), func=AF.Exp,
                                        scale=nh1[:]))
            s_r1 = cnt["s"]

            @blk.vector
            def _(v):
                v.reciprocal(out=R1c, in_=R1b)
                v.drain()
                v.tensor_tensor(out=R1a, in0=R1a, in1=R1c, op=ALU.mult)
                v.drain()
                einc("v", v.tensor_scalar(out=R1a, in0=R1a, scalar1=1e-5,
                                          scalar2=None, op0=ALU.add))
                v.drain()
            v_u2 = cnt["v"]
            if fill is not None:
                fill(2)

            @blk.scalar
            def _(sc):
                sc.wait_ge(sems["v"], v_u2)
                sc.wait_ge(sems["s"], s_r1)
                sc.activation(out=R1b, in_=R1a, func=AF.Ln)
                sc.drain()
                sc.activation(out=ST(2), in_=R1b, func=AF.Exp, scale=nh1[:])
                einc("s", sc.activation(out=R1c, in_=R1b, func=AF.Exp,
                                        scale=ph1[:]))
            s_sv = cnt["s"]

            @blk.vector
            def _(v):
                v.wait_ge(sems["s"], s_sv)
                v.tensor_tensor(out=ST(1), in0=ST(1), in1=ST(2), op=ALU.mult)
                v.drain()
                einc("v", v.tensor_tensor(out=ST(0), in0=ST(0), in1=ST(1),
                                          op=ALU.mult))
            v_ab2 = cnt["v"]
            if fill is not None:
                fill(1)

            t_rp = cnt["t"]
            s_rp = cnt["s"]

            @blk.tensor
            def _(t):
                t.wait_ge(sems["v"], v_ab2)
                for j, srcap in enumerate((ST(1), ST(0), R1c)):
                    if j >= 2:
                        t.wait_ge(sems["s"], s_rp + j - 1)
                    lhs = ones_rf[:] if j < 2 else ones5[32:33, 0:P]
                    einc("t", t.matmul((pa if j % 2 == 0 else pc)[:],
                                       lhs, srcap,
                                       start=True, stop=True))

            @blk.scalar
            def _(sc):
                if ctx is not None:
                    sc.wait_ge(sems["t"], ctx["t_mxr2"])
                    sc.wait_ge(sems["v"], ctx["v_s2"])
                for j in range(3):
                    sc.wait_ge(sems["t"], t_rp + j + 1)
                    einc("s", sc.copy(out=reps[j][:],
                                      in_=(pa if j % 2 == 0 else pc)[:]))
            s_reps = cnt["s"]
            if fill is not None:
                fill(1)

            @blk.vector
            def _(v):
                v.wait_ge(sems["s"], s_reps)
                v.tensor_tensor(out=H1[:, :, 0:TS], in0=xs[:, :, 0:TS],
                                in1=bc(reps[0]), op=ALU.mult)
                v.drain()
                einc("v", v.tensor_tensor(out=H1[:, :, 0:TS],
                                          in0=H1[:, :, 0:TS],
                                          in1=bc(reps[1]), op=ALU.subtract))
                v.drain()
            v_h1f = cnt["v"]
            return {"t_sB": t_sB, "s_reps": s_reps, "v_h1f": v_h1f}

    def attn_rest(blk, s, fr, fm, fill=None):
        if True:
            xs = X[s % 2]
            lo = s * TS
            lda_b = [cnt["lda0"], cnt["lda1"]]
            wfa_b = cnt["wfa"]

            @blk.sync
            def _(sy):
                sy.wait_ge(sems["ldfx"], p1_sp0)
                sy.wait_ge(sems["spy"], p1_sp1)
                if fm is not None:
                    sy.wait_ge(sems["t"], fm["t_f1e"])
                for j in range(12):
                    wi, hf = divmod(j, 2)
                    if wfa_b + j - 1 > 0:
                        sy.wait_ge(sems["wfa"], max(0, wfa_b + j - 1))
                    dinc("lda%d" % (j % 2), sy.dma_start(
                        out=WS[j % 2][:, :, :],
                        in_=at_s[wi, :, :, hf * TS:(hf + 1) * TS]))

            ST = lambda k: wqsf[0:1, k * TS:(k + 1) * TS]

            @blk.vector
            def _(v):
                v.wait_ge(sems["s"], fr["s_reps"])
                for ccx in range(CC):
                    v.tensor_scalar(out=HB[:, ccx, :], in0=H1[:, ccx, 0:TS],
                                    scalar1=gvv[:, ccx:ccx + 1],
                                    scalar2=bvv[:, ccx:ccx + 1],
                                    op0=ALU.mult, op1=ALU.add)
                v.drain()
                v.tensor_tensor(out=H1[:, :, 0:TS], in0=H1[:, :, 0:TS],
                                in1=bc(reps[2]), op=ALU.mult)
                v.drain()
                einc("v", v.tensor_copy(out=OB[:, :, :], in_=fl512(H1)))
            v_h1 = cnt["v"]
            if fill is not None:
                fill(1)

            t_kvg = cnt["t"]
            s_kvg_b = cnt["s"]

            @blk.tensor
            def _(t):
                for j in range(6):
                    wi, hf = divmod(j, 2)
                    t.wait_ge(sems["lda%d" % (j % 2)],
                              lda_b[j % 2] + 16 * (j // 2 + 1))
                    t.wait_ge(sems["v"], v_h1)
                    for dcl in range(4):
                        gidx = j * 4 + dcl
                        if gidx >= 4:
                            t.wait_ge(sems["s"], s_kvg_b + gidx - 3)
                        mvt = HB if wi < 2 else OB
                        for ccx in range(CC):
                            ii = t.matmul(pb[gidx % 4][:],
                                          WS[j % 2][:, ccx, dcl * P:(dcl + 1) * P],
                                          mvt[:, ccx, :], start=(ccx == 0),
                                          stop=(ccx == CC - 1))
                        einc("t", ii)

            @blk.scalar
            def _(sc):
                if fm is not None:
                    sc.wait_ge(sems["sout"], fm["sout_e"])
                for j in range(6):
                    wi, hf = divmod(j, 2)
                    for dcl in range(4):
                        dc = hf * 4 + dcl
                        gidx = j * 4 + dcl
                        sc.wait_ge(sems["t"], t_kvg + gidx + 1)
                        if wi == 0:
                            ii = sc.copy(out=KF[:, dc, 1:513],
                                         in_=pb[gidx % 4][:])
                        elif wi == 1:
                            ii = sc.copy(out=VS[:, dc, 0:TS],
                                         in_=pb[gidx % 4][:])
                        else:
                            ii = sc.activation(out=GB[:, dc, :],
                                               in_=pb[gidx % 4][:],
                                               func=AF.Sigmoid)
                        einc("s", ii)
                        if dcl == 3:
                            sc.drain().then_inc(sems["wfa"], 1)
                            cnt["wfa"] += 1
            s_kvg = cnt["s"]
            if fill is not None:
                fill(1)

            @blk.vector
            def _(v):
                v.wait_ge(sems["s"], s_kvg_b + 8)
                einc("v", v.tensor_copy(out=KB[:, :, :], in_=KF[:, :, 1:513]))
            v_kb = cnt["v"]
            if fill is not None:
                fill(1)

            t_y = cnt["t"]
            s_y_b = cnt["s"]

            ybk = (lambda gidx: pb[4 + gidx % 4]) if s > 0 else \
                (lambda gidx: pb[gidx % 4])

            @blk.tensor
            def _(t):
                for j in range(6, 8):
                    t.wait_ge(sems["lda%d" % (j % 2)],
                              lda_b[j % 2] + 16 * (j // 2 + 1))
                    t.wait_ge(sems["v"], v_kb)
                    for dcl in range(4):
                        gidx = (j - 6) * 4 + dcl
                        if gidx >= 4:
                            t.wait_ge(sems["s"], s_y_b + gidx - 3)
                        elif s == 0:
                            t.wait_ge(sems["s"], s_kvg)
                        for ccx in range(CC):
                            ii = t.matmul(ybk(gidx)[:],
                                          WS[j % 2][:, ccx, dcl * P:(dcl + 1) * P],
                                          KB[:, ccx, :], start=(ccx == 0),
                                          stop=(ccx == CC - 1))
                        einc("t", ii)

            @blk.scalar
            def _(sc):
                for gidx in range(8):
                    sc.wait_ge(sems["t"], t_y + gidx + 1)
                    ii = sc.copy(out=Y0[:, gidx, 0:TS], in_=ybk(gidx)[:])
                    einc("s", ii)
                    if gidx % 4 == 3:
                        sc.drain().then_inc(sems["wfa"], 1)
                        cnt["wfa"] += 1
            s_y0 = cnt["s"]
            if fill is not None:
                fill(1)

            @blk.vector
            def _(v):
                v.wait_ge(sems["s"], s_y_b + 4)
                v.tensor_tensor(out=VS[:, 0:4, 0:TS], in0=Y0[:, 0:4, 0:TS],
                                in1=VS[:, 0:4, 0:TS], op=ALU.subtract)
                v.drain()
                v.tensor_tensor(out=KF[:, 0:4, 1:513], in0=VS[:, 0:4, 0:TS],
                                in1=KF[:, 0:4, 1:513], op=ALU.mult)
                v.wait_ge(sems["s"], s_y0)
                v.tensor_tensor(out=VS[:, 4:8, 0:TS], in0=Y0[:, 4:8, 0:TS],
                                in1=VS[:, 4:8, 0:TS], op=ALU.subtract)
                v.drain()
                v.tensor_tensor(out=KF[:, 4:8, 1:513],
                                in0=VS[:, 4:8, 0:TS],
                                in1=KF[:, 4:8, 1:513],
                                op=ALU.mult)
                v.tensor_copy(out=KF[:, :, 0], in_=gtail[:])
                v.drain()
                for ccx in range(CC):
                    v.tensor_tensor_scan(out=VS[:, ccx, 0:TS], data0=ones5[:],
                                         data1=KF[:, ccx, 0:TS],
                                         initial=carry[:, ccx:ccx + 1],
                                         op0=ALU.mult, op1=ALU.add)
                v.drain()
                v.tensor_copy(out=carry[:], in_=VS[:, :, 511])
                einc("v", v.tensor_copy(out=gtail[:], in_=KF[:, :, 512]))
            v_gs = cnt["v"]
            if fill is not None:
                fill(1)

            t_q = cnt["t"]
            v_q_b = cnt["v"]

            @blk.tensor
            def _(t):
                for j in range(8, 10):
                    t.wait_ge(sems["lda%d" % (j % 2)],
                              lda_b[j % 2] + 16 * (j // 2 + 1))
                    t.wait_ge(sems["s"], s_y0 if s == 0 else s_kvg)
                    t.wait_ge(sems["v"], v_h1)
                    for dcl in range(4):
                        gidx = (j - 8) * 4 + dcl
                        if gidx >= 4:
                            t.wait_ge(sems["v"], v_q_b + gidx - 3)
                        for ccx in range(CC):
                            ii = t.matmul(pb[gidx % 4][:],
                                          WS[j % 2][:, ccx, dcl * P:(dcl + 1) * P],
                                          HB[:, ccx, :], start=(ccx == 0),
                                          stop=(ccx == CC - 1))
                        einc("t", ii)

            @blk.vector
            def _(v):
                v.wait_ge(sems["v"], mark_lr)
                v.wait_ge(sems["v"], v_gs)
                for gidx in range(8):
                    v.wait_ge(sems["t"], t_q + gidx + 1)
                    ii = v.scalar_tensor_tensor(
                        out=KF[:, gidx, 1:513], in0=pb[gidx % 4][:],
                        scalar=lrv[:, gidx:gidx + 1], in1=VS[:, gidx, 0:TS],
                        op0=ALU.mult, op1=ALU.mult)
                    einc("v", ii)
                    if gidx % 4 == 3:
                        v.drain().then_inc(sems["wfa"], 1)
                        cnt["wfa"] += 1
                v.drain()
                v.tensor_tensor(out=KF[:, :, 1:513], in0=KF[:, :, 1:513],
                                in1=pos[:, None, lo:lo + TS]
                                .broadcast_to([P, CC, TS]), op=ALU.mult)
                v.drain()
                v.tensor_tensor(out=fl512(Y0), in0=fl512(Y0),
                                in1=KF[:, :, 1:513],
                                op=ALU.subtract)
                v.drain()
                einc("v", v.tensor_tensor(out=OB[:, :, :], in0=fl512(Y0),
                                          in1=GB[:, :, :], op=ALU.mult))
            v_o1 = cnt["v"]
            if fill is not None:
                fill(1)

            t_o = cnt["t"]
            v_wo_b = cnt["v"]

            @blk.tensor
            def _(t):
                for j in range(10, 12):
                    t.wait_ge(sems["lda%d" % (j % 2)],
                              lda_b[j % 2] + 16 * (j // 2 + 1))
                    t.wait_ge(sems["v"], v_o1)
                    for dcl in range(4):
                        gidx = (j - 10) * 4 + dcl
                        if gidx >= 4:
                            t.wait_ge(sems["v"], v_wo_b + gidx - 3)
                        for ccx in range(CC):
                            ii = t.matmul(pb[gidx % 4][:],
                                          WS[j % 2][:, ccx, dcl * P:(dcl + 1) * P],
                                          OB[:, ccx, :], start=(ccx == 0),
                                          stop=(ccx == CC - 1))
                        einc("t", ii)

            @blk.vector
            def _(v):
                for gidx in range(8):
                    v.wait_ge(sems["t"], t_o + gidx + 1)
                    ii = v.tensor_scalar(out=VS[:, gidx, 0:TS],
                                         in0=pb[gidx % 4][:],
                                         scalar1=lsv[:, gidx:gidx + 1],
                                         scalar2=None, op0=ALU.mult)
                    einc("v", ii)
                    if gidx % 4 == 3:
                        v.drain().then_inc(sems["wfa"], 1)
                        cnt["wfa"] += 1
                v.drain()
                v.tensor_tensor(out=fl512(VS), in0=fl512(VS), in1=fl512(H1),
                                op=ALU.add)
                v.drain()
                einc("v", v.tensor_tensor(out=fl512(xs), in0=fl512(VS),
                                          in1=fl512(xs), op=ALU.add))
                v.drain()
            v_x2 = cnt["v"]
            if fill is not None:
                fill(1)

            @blk.scalar
            def _(sc):
                sc.wait_ge(sems["v"], v_x2)
                sc.wait_ge(sems["spl"], cnt["spl"])
                dinc("spl", sc.dma_start(out=x2_s[:, :, lo:lo + TS],
                                         in_=fl512(xs)))
            spl_x2 = cnt["spl"]

            @blk.vector
            def _(v):
                v.wait_ge(sems["v"], v_x2)
                einc("v", v.tensor_copy(out=XS[:, :, :], in_=fl512(xs)))
            v_x2b = cnt["v"]
            if fill is not None:
                fill(1)

            @blk.tensor
            def _(t):
                t.wait_ge(sems["v"], v_x2b)
                for ccx in range(CC):
                    ii = t.matmul(pb[2][0:1, :], ones_c[:], XS[:, ccx, :],
                                  start=(ccx == 0), stop=(ccx == CC - 1))
                einc("t", ii)
            t_3A = cnt["t"]

            @blk.scalar
            def _(sc):
                sc.wait_ge(sems["v"], v_x2b)
                einc("s", sc.activation(out=KB[:, :, :], in_=fl512(xs),
                                        func=AF.Square))
            s_3sq = cnt["s"]

            @blk.tensor
            def _(t):
                t.wait_ge(sems["s"], s_3sq)
                for ccx in range(CC):
                    ii = t.matmul(pb[3][0:1, :], ones_c[:], KB[:, ccx, :],
                                  start=(ccx == 0), stop=(ccx == CC - 1))
                einc("t", ii)
            t_3B = cnt["t"]

            @blk.vector
            def _(v):
                v.wait_ge(sems["t"], t_3B)
                v.tensor_scalar(out=ST(0), in0=pb[2][0:1, :], scalar1=1.0 / C,
                                scalar2=None, op0=ALU.mult)
                v.tensor_scalar(out=ST(1), in0=pb[3][0:1, :], scalar1=1.0 / C,
                                scalar2=None, op0=ALU.mult)
                v.drain()
                v.tensor_tensor(out=ST(3), in0=ST(0), in1=ST(0), op=ALU.mult)
                v.drain()
                v.tensor_tensor(out=ST(1), in0=ST(1), in1=ST(3),
                                op=ALU.subtract)
                v.drain()
                einc("v", v.tensor_scalar(out=ST(2), in0=ST(1), scalar1=1e-5,
                                          scalar2=None, op0=ALU.add))
                v.drain()
            v_3u = cnt["v"]
            if fill is not None:
                fill(1)

            @blk.scalar
            def _(sc):
                sc.wait_ge(sems["v"], v_3u)
                sc.activation(out=ST(3), in_=ST(2), func=AF.Ln)
                sc.drain()
                einc("s", sc.activation(out=ST(1), in_=ST(3), func=AF.Exp,
                                        scale=nh1[:]))
            s_r3 = cnt["s"]

            t_3r = cnt["t"]

            @blk.tensor
            def _(t):
                t.wait_ge(sems["s"], s_r3)
                einc("t", t.matmul(pb[2][:], ones_rf[:], ST(0), start=True,
                                   stop=True))
                einc("t", t.matmul(pb[3][:], ones_rf[:], ST(1), start=True,
                                   stop=True))

            @blk.scalar
            def _(sc):
                sc.wait_ge(sems["t"], t_3r + 2)
                sc.copy(out=reps[0][:], in_=pb[2][:])
                einc("s", sc.copy(out=reps[1][:], in_=pb[3][:]))
            s_3rep = cnt["s"]
            if fill is not None:
                fill(1)

            @blk.vector
            def _(v):
                v.wait_ge(sems["s"], s_3rep)
                v.tensor_tensor(out=H1[:, :, 0:TS], in0=xs[:, :, 0:TS],
                                in1=bc(reps[0]), op=ALU.subtract)
                v.drain()
                v.scalar_tensor_tensor(out=Y0[:, :, 0:TS], in0=H1[:, :, 0:TS],
                                       scalar=-1.0, in1=H1[:, :, 0:TS],
                                       op0=ALU.mult, op1=ALU.max)
                v.drain()
                v.tensor_tensor(out=Y0[:, 0:4, 0:TS], in0=Y0[:, 0:4, 0:TS],
                                in1=Y0[:, 4:8, 0:TS], op=ALU.max)
                v.drain()
                v.tensor_tensor(out=Y0[:, 0:2, 0:TS], in0=Y0[:, 0:2, 0:TS],
                                in1=Y0[:, 2:4, 0:TS], op=ALU.max)
                v.drain()
                einc("v", v.tensor_tensor(out=rmx[:], in0=Y0[:, 0, 0:TS],
                                          in1=Y0[:, 1, 0:TS], op=ALU.max))
            v_rmx = cnt["v"]
            if fill is not None:
                fill(1)

            t_tp = cnt["t"]
            v_tp_b = cnt["v"]
            tp2bank = (pb[3], pb[4], pb[5], pb[2])

            @blk.tensor
            def _(t):
                t.wait_ge(sems["v"], v_rmx)
                for j in range(4):
                    if j >= 1:
                        t.wait_ge(sems["v"], v_tp_b + j)
                        einc("t", t.transpose(
                            tp2bank[j - 1][0:1, (j - 1) * P:j * P],
                            smx[:, j - 1:j], idf[:]))
                    einc("t", t.transpose(pb[2][:, 0:P],
                                          rmx[:, j * P:(j + 1) * P], idf[:]))
                t.wait_ge(sems["v"], v_tp_b + 4)
                einc("t", t.transpose(tp2bank[3][0:1, 3 * P:4 * P],
                                      smx[:, 3:4], idf[:]))

            @blk.vector
            def _(v):
                for j in range(4):
                    v.wait_ge(sems["t"], t_tp + 2 * j + 1)
                    ii = v.reduce_max(out=smx[:, j:j + 1], in_=pb[2][:, 0:P],
                                      axis=AX.X)
                    einc("v", ii)
            v_smx = cnt["v"]
            t_tp8 = cnt["t"]

            @blk.scalar
            def _(sc):
                sc.wait_ge(sems["t"], t_tp8)
                for j in range(4):
                    ii = sc.copy(
                        out=wqsf[0:1, 3 * TS + j * P:3 * TS + (j + 1) * P],
                        in_=tp2bank[j][0:1, j * P:(j + 1) * P])
                einc("s", ii)
                sc.drain()
            s_mrow = cnt["s"]

            t_mxr = cnt["t"]

            @blk.tensor
            def _(t):
                t.wait_ge(sems["s"], s_mrow)
                einc("t", t.matmul(pb[2][:], ones_rf[:],
                                   wqsf[0:1, 3 * TS:4 * TS],
                                   start=True, stop=True))

            @blk.scalar
            def _(sc):
                sc.wait_ge(sems["t"], t_mxr + 1)
                einc("s", sc.copy(out=reps[2][:], in_=pb[2][:]))
                sc.drain()
            s_mx = cnt["s"]
            if fill is not None:
                fill(1)

            @blk.vector
            def _(v):
                v.wait_ge(sems["s"], s_mx)
                v.tensor_tensor(out=reps[2][:], in0=reps[2][:], in1=reps[1][:],
                                op=ALU.mult)
                v.drain()
                v.tensor_scalar(out=reps[2][:], in0=reps[2][:], scalar1=1e-5,
                                scalar2=None, op0=ALU.max)
                v.drain()
                v.tensor_scalar(out=qc2[:], in0=reps[2][:], scalar1=1.0 / 127,
                                scalar2=None, op0=ALU.mult)
                v.reciprocal(out=reps[3][:], in_=reps[2][:])
                v.drain()
                v.scalar_tensor_tensor(out=reps[3][:], in0=reps[1][:],
                                       scalar=127.0, in1=reps[3][:],
                                       op0=ALU.mult, op1=ALU.mult)
                v.drain()
                einc("v", v.tensor_tensor(out=H1[:, :, 0:TS],
                                          in0=H1[:, :, 0:TS],
                                          in1=bc(reps[3]), op=ALU.mult))
                v.drain()
            v_qq = cnt["v"]
            if fill is not None:
                fill(1)

            v_q1_b = cnt["v"]

            @blk.vector
            def _(v):
                v.wait_ge(sems["t"], fr["t_sB"])
                v.wait_ge(sems["spl"], spl_x2)
                NP = CC // 2
                qiv = [qiw, qiw2]
                qcv = [qcw, qcw2]
                c2v = [qd2.rearrange("p (a b) -> p a b", b=TS),
                       flat(xs)[:, 0:2 * TS].rearrange(
                           "p (a b) -> p a b", b=TS)]

                def ap(k):
                    p_ = k % 2
                    return (H1[:, 2 * k:2 * k + 2, 0:TS],
                            XQ[:, 2 * k:2 * k + 2, :],
                            qiv[p_][:, :, :], qcv[p_][:, :, :], c2v[p_])

                def s_i16(k):
                    hp, xp, qi, qc, c2 = ap(k)
                    v.tensor_scalar(out=qi, in0=hp, scalar1=-0.5,
                                    scalar2=None, op0=ALU.add)

                def s_c12(k):
                    hp, xp, qi, qc, c2 = ap(k)
                    v.scalar_tensor_tensor(out=qc, in0=hp, scalar=-0.5,
                                           in1=qi, op0=ALU.add, op1=ALU.is_gt)
                    v.scalar_tensor_tensor(out=c2, in0=hp, scalar=-1.5,
                                           in1=qi, op0=ALU.add, op1=ALU.is_gt)

                s_i16(0)
                v.drain()
                s_c12(0)
                v.drain()
                for k in range(NP):
                    hp, xp, qi, qc, c2 = ap(k)
                    v.tensor_tensor(out=xp, in0=qi, in1=qc, op=ALU.add)
                    if k + 1 < NP:
                        s_i16(k + 1)
                    v.drain()
                    v.scalar_tensor_tensor(out=qc, in0=hp, scalar=0.5,
                                           in1=qi, op0=ALU.add, op1=ALU.is_lt)
                    v.tensor_tensor(out=xp, in0=xp, in1=c2, op=ALU.add)
                    if k + 1 < NP:
                        s_c12(k + 1)
                    v.drain()
                    ii = v.tensor_tensor(out=xp, in0=xp, in1=qc,
                                         op=ALU.subtract)
                    einc("v", ii)
                    v.drain()
            v_xq1 = cnt["v"]
            return {"v_xq1": v_xq1, "v_q1_b": v_q1_b,
                    "wfa_end": cnt["wfa"]}

    def ffn_seg(blk, s, am, fill=None, front_cb=None, pfm=None):
        if True:
            lo = s * TS
            n = TS if s > 0 else TS - 1
            wlo = lo - 1 if s > 0 else 0
            j0 = 0 if s > 0 else 1
            last = (s == NSEG - 1)
            spl_done = cnt["spl"]
            sout_prev = cnt["sout"]
            ldf_b = {"10": cnt["ldf10"], "11": cnt["ldf11"],
                     "20": cnt["ldf20"], "21": cnt["ldf21"]}
            wff_b = cnt["wff"]

            @blk.sync
            def _(sy):
                sy.wait_ge(sems["spl"], spl_done)
                sy.wait_ge(sems["wfa"], am["wfa_end"])
                if s >= 1 and s + 1 < NSEG:
                    sy.wait_ge(sems["v"], pfm["v_cv"])
                    sy.wait_ge(sems["s"], pfm["sgm7"])
                    dinc("ldx%d" % ((s + 1) % 2), sy.dma_start(
                        out=fl512(X[(s + 1) % 2]),
                        in_=xin[:, :, lo + TS:lo + 2 * TS]))
                if s == 0:
                    sy.wait_ge(sems["sp0"], pm["sp0_w1"])
                    sy.wait_ge(sems["sp1"], pm["sp1_w1"])
                for gi in range(8):
                    if gi >= 2:
                        sy.wait_ge(sems["wff"], wff_b + gi - 1)
                    elif wff_b + gi - 9 > 0:
                        sy.wait_ge(sems["wff"], wff_b + gi - 9)
                    dinc("ldf1%d" % (gi % 2), sy.dma_start(
                        out=W1U[gi % 2][:, :, :], in_=w1_s[gi, :, :, :]))

            def emit_ldf2():
                @blk.sync
                def _(sy):
                    if s == 0:
                        sy.wait_ge(sems["sp0"], pm["sp0_all"])
                        sy.wait_ge(sems["sp1"], pm["sp1_all"])
                    for dc in range(CC):
                        if dc < 2:
                            sy.wait_ge(sems["wff"], wff_b + 8)
                        else:
                            sy.wait_ge(sems["wff"], wff_b + 7 + dc)
                        dinc("ldf2%d" % (dc % 2), sy.dma_start(
                            out=w2view(W2U[dc % 2]), in_=w2_s[dc, :, :, :]))

            x2w_b = cnt["x2w"]

            @blk.scalar
            def _(sc):
                sc.wait_ge(sems["spl"], spl_done)
                dinc("x2w", sc.dma_start(out=X2W[:, :, 0:n],
                                         in_=x2_s[:, :, wlo:wlo + n]))
                if last:
                    dinc("x2w", sc.dma_start(out=ttmp[:],
                                             in_=x2_s[:, :, T - 1],
                                             allow_slow_non_contiguous=True))
            x2w_e = cnt["x2w"]

            stx = [flat(X[s % 2])[:, 516 * i:516 * i + 516] for i in range(7)]
            stx.append(wqsf[:, 0:516])
            tg = lambda g: [H1[:, (g % 2) * 4 + j, 0:TS] for j in range(4)]
            tbv = [VS[:, j, 0:TS] for j in range(4)]

            t_f1 = cnt["t"]
            s_f_b = cnt["s"]
            egm, sgm = {}, {}
            k = s_f_b
            for g in range(8):
                k += 1
                egm[g] = k
                if g >= 1:
                    k += 1
                    sgm[g - 1] = k
            k += 1
            sgm[7] = k
            v_cg_b = cnt["v"]

            fbk = lambda g, hc: pb[(4 if g % 2 == 0 else 0) + hc % 4]

            @blk.tensor
            def _(t):
                for gi in range(8):
                    if gi == 0 and s == 0:
                        t.wait_ge(sems["s"], pm["s_sw"])
                    if gi == 1:
                        t.wait_ge(sems["v"], am["v_xq1"])
                    t.wait_ge(sems["ldf1%d" % (gi % 2)],
                              ldf_b["1%d" % (gi % 2)] + 16 * (gi // 2 + 1))
                    if gi >= 2:
                        t.wait_ge(sems["s"], egm[gi - 2])
                    for hi in range(4):
                        hc = gi * 4 + hi
                        for ccx in range(CC):
                            if gi == 0 and hi == 0:
                                t.wait_ge(sems["v"],
                                          am["v_q1_b"] + ccx // 2 + 1)
                            ii = t.matmul(fbk(gi, hc)[:],
                                          W1U[gi % 2][:, ccx, hi * P:(hi + 1) * P],
                                          XQ[:, ccx, :], start=(ccx == 0),
                                          stop=(ccx == CC - 1))
                        einc("t", ii)
                    t.wait_ge(sems["t"], cnt["t"])
                    t.nop().then_inc(sems["wff"], 1)
                    cnt["wff"] += 1
            t_f1e = cnt["t"]

            cg_mark = {}
            scl_mark = {}

            def emit_scale(g):
                @blk.vector
                def _(v, g=g):
                    v.wait_ge(sems["t"], t_f1 + 4 * g + 4)
                    for hi in range(4):
                        hc = 4 * g + hi
                        ii = v.tensor_tensor(out=fbk(g, hc)[:],
                                             in0=fbk(g, hc)[:],
                                             in1=qc2[:], op=ALU.mult)
                    einc("v", ii)
                    v.drain()
                scl_mark[g] = cnt["v"]

            emit_scale(0)
            emit_scale(1)
            for g in range(8):
                @blk.scalar
                def _(sc, g=g):
                    if g == 0 and s == 0:
                        sc.wait_ge(sems["v"], pm["v_cwv"])
                    sc.wait_ge(sems["v"], scl_mark[g])
                    if g >= 2:
                        sc.wait_ge(sems["v"], cg_mark[g - 2])
                    for hi in range(4):
                        hc = 4 * g + hi
                        st = stx[hc % 8]
                        sc.copy(out=st[:, 0:2], in_=tails[:, hc, :])
                        sc.copy(out=st[:, 2:514], in_=fbk(g, hc)[:])
                    sc.drain()
                    for hi in range(4):
                        hc = 4 * g + hi
                        ii = sc.copy(out=tails[:, hc, :],
                                     in_=fbk(g, hc)[:, 510:512])
                    einc("s", ii)
                    if g >= 1:
                        gq = g - 1
                        sc.wait_ge(sems["v"], cg_mark[gq])
                        for hi in range(4):
                            hc = 4 * gq + hi
                            ii = sc.activation(out=silu_ap(hc, n),
                                               in_=tg(gq)[hc % 4][0:P, 0:n],
                                               func=AF.Silu,
                                               bias=cbv[:, hc:hc + 1])
                        einc("s", ii)
                    sc.drain()

                @blk.vector
                def _(v, g=g):
                    if g == 0:
                        v.wait_ge(sems["sout"], sout_prev)
                    v.wait_ge(sems["s"], egm[g])
                    for hi in range(4):
                        hc = 4 * g + hi
                        st = stx[hc % 8]
                        v.tensor_scalar(out=tg(g)[hc % 4][0:P, 0:n],
                                        in0=st[:, j0:j0 + n],
                                        scalar1=cwv[:, 3 * hc:3 * hc + 1],
                                        scalar2=None, op0=ALU.mult)
                    v.drain()
                    for hi in range(4):
                        hc = 4 * g + hi
                        st = stx[hc % 8]
                        v.scalar_tensor_tensor(
                            out=tbv[hc % 4][0:P, 0:n],
                            in0=st[:, j0 + 1:j0 + 1 + n],
                            scalar=cwv[:, 3 * hc + 1:3 * hc + 2],
                            in1=tg(g)[hc % 4][0:P, 0:n],
                            op0=ALU.mult, op1=ALU.add)
                    v.drain()
                    for hi in range(4):
                        hc = 4 * g + hi
                        st = stx[hc % 8]
                        ii = v.scalar_tensor_tensor(
                            out=tg(g)[hc % 4][0:P, 0:n],
                            in0=st[:, j0 + 2:j0 + 2 + n],
                            scalar=cwv[:, 3 * hc + 2:3 * hc + 3],
                            in1=tbv[hc % 4][0:P, 0:n],
                            op0=ALU.mult, op1=ALU.add)
                    einc("v", ii)
                    v.drain()
                cg_mark[g] = cnt["v"]
                if g + 2 <= 7:
                    emit_scale(g + 2)
                if fill is not None:
                    fill(1)

            @blk.scalar
            def _(sc):
                gq = 7
                sc.wait_ge(sems["v"], cg_mark[gq])
                sc.drain()
                for hi in range(4):
                    hc = 4 * gq + hi
                    ii = sc.activation(out=silu_ap(hc, n),
                                       in_=tg(gq)[hc % 4][0:P, 0:n],
                                       func=AF.Silu,
                                       bias=cbv[:, hc:hc + 1])
                einc("s", ii)
            s_si_all = cnt["s"]
            assert s_si_all == sgm[7]
            v_cv_all = cnt["v"]

            if s == 0:
                @blk.sync
                def _(sy):
                    sy.wait_ge(sems["v"], v_cv_all)
                    sy.wait_ge(sems["s"], sgm[7])
                    dinc("ldx1", sy.dma_start(
                        out=fl512(X[1]),
                        in_=xin[:, :, lo + TS:lo + 2 * TS]))
            emit_ldf2()

            if last:
                cw3 = cwv[:].rearrange("p (a b) -> p a b", b=3)

                @blk.vector
                def _(v):
                    v.wait_ge(sems["s"], s_si_all)
                    v.tensor_tensor(out=stail[:], in0=tails[:, :, 0],
                                    in1=cw3[:, :, 0], op=ALU.mult)
                    v.drain()
                    v.tensor_tensor(out=hmt[0][:, 0:HC], in0=tails[:, :, 1],
                                    in1=cw3[:, :, 1], op=ALU.mult)
                    v.drain()
                    v.tensor_tensor(out=stail[:], in0=stail[:],
                                    in1=hmt[0][:, 0:HC], op=ALU.add)
                    v.drain()
                    einc("v", v.tensor_tensor(out=stail[:], in0=stail[:],
                                              in1=cbv[:], op=ALU.add))
                    v.drain()
                v_tc = cnt["v"]

                @blk.scalar
                def _(sc):
                    sc.wait_ge(sems["v"], v_tc)
                    einc("s", sc.activation(out=stail[:], in_=stail[:],
                                            func=AF.Silu))
                    sc.drain()
                s_ts = cnt["s"]

            @blk.vector
            def _(v):
                for u in range(4):
                    v.wait_ge(sems["s"], sgm[2 * u + 1])
                    v.scalar_tensor_tensor(
                        out=Y0[:, :, 0:n],
                        in0=SILU[u][:, :, 0:n], scalar=-1.0,
                        in1=SILU[u][:, :, 0:n],
                        op0=ALU.mult, op1=ALU.max)
                    v.drain()
                    v.tensor_tensor(out=Y0[:, 0:4, 0:n], in0=Y0[:, 0:4, 0:n],
                                    in1=Y0[:, 4:8, 0:n], op=ALU.max)
                    v.drain()
                    v.tensor_tensor(out=Y0[:, 0:2, 0:n], in0=Y0[:, 0:2, 0:n],
                                    in1=Y0[:, 2:4, 0:n], op=ALU.max)
                    v.drain()
                    if u == 0:
                        if n < TS:
                            v.memset(rmx[:], 0.0)
                            v.drain()
                        ii = v.tensor_tensor(out=rmx[:, 0:n],
                                             in0=Y0[:, 0, 0:n],
                                             in1=Y0[:, 1, 0:n], op=ALU.max)
                    else:
                        v.tensor_tensor(out=Y0[:, 0, 0:n], in0=Y0[:, 0, 0:n],
                                        in1=Y0[:, 1, 0:n], op=ALU.max)
                        v.drain()
                        ii = v.tensor_tensor(out=rmx[:, 0:n], in0=rmx[:, 0:n],
                                             in1=Y0[:, 0, 0:n], op=ALU.max)
                    einc("v", ii)
                    v.drain()
            v_rm2 = cnt["v"]

            t_tp = cnt["t"]
            v_tp_b = cnt["v"]
            tp2bank = (pb[3], pb[4], pb[5], pb[2])

            @blk.tensor
            def _(t):
                t.wait_ge(sems["v"], v_rm2)
                for j in range(4):
                    if j >= 1:
                        t.wait_ge(sems["v"], v_tp_b + j)
                        einc("t", t.transpose(
                            tp2bank[j - 1][0:1, (j - 1) * P:j * P],
                            smx[:, j - 1:j], idf[:]))
                    einc("t", t.transpose(pb[2][:, 0:P],
                                          rmx[:, j * P:(j + 1) * P], idf[:]))
                t.wait_ge(sems["v"], v_tp_b + 4)
                einc("t", t.transpose(tp2bank[3][0:1, 3 * P:4 * P],
                                      smx[:, 3:4], idf[:]))

            @blk.vector
            def _(v):
                for j in range(4):
                    v.wait_ge(sems["t"], t_tp + 2 * j + 1)
                    ii = v.reduce_max(out=smx[:, j:j + 1], in_=pb[2][:, 0:P],
                                      axis=AX.X)
                    einc("v", ii)
            v_sm2 = cnt["v"]
            t_tp8 = cnt["t"]

            @blk.scalar
            def _(sc):
                sc.wait_ge(sems["t"], t_tp8)
                for j in range(4):
                    ii = sc.copy(
                        out=wqsf[0:1, 3 * TS + j * P:3 * TS + (j + 1) * P],
                        in_=tp2bank[j][0:1, j * P:(j + 1) * P])
                einc("s", ii)
                sc.drain()
            s_mr2 = cnt["s"]

            t_mxr2 = cnt["t"]

            @blk.tensor
            def _(t):
                t.wait_ge(sems["s"], s_mr2)
                einc("t", t.matmul(pb[2][:], ones_rf[:],
                                   wqsf[0:1, 3 * TS:4 * TS],
                                   start=True, stop=True))

            @blk.scalar
            def _(sc):
                sc.wait_ge(sems["t"], t_mxr2 + 1)
                einc("s", sc.copy(out=reps[2][:], in_=pb[2][:]))
                sc.drain()
            s_mx2 = cnt["s"]

            @blk.vector
            def _(v):
                v.wait_ge(sems["s"], s_mx2)
                v.tensor_scalar(out=reps[2][:], in0=reps[2][:], scalar1=1e-5,
                                scalar2=None, op0=ALU.max)
                v.drain()
                v.tensor_scalar(out=reps[3][:], in0=reps[2][:],
                                scalar1=1.0 / 127, scalar2=None, op0=ALU.mult)
                v.drain()
                v.reciprocal(out=reps[2][:], in_=reps[2][:])
                v.drain()
                einc("v", v.tensor_scalar(out=repb[:], in0=reps[2][:],
                                          scalar1=127.0, scalar2=None,
                                          op0=ALU.mult))
                v.drain()
            v_s2 = cnt["v"]

            @blk.vector
            def _(v):
                v.wait_ge(sems["v"], v_s2)
                NP = HC // 2
                hmv = [wqsf[:, 4 * TS:6 * TS].rearrange(
                           "p (a b) -> p a b", b=TS),
                       flat(X[s % 2])[:, 0:2 * TS].rearrange(
                           "p (a b) -> p a b", b=TS)]
                qiv = [qiw, qiw2]
                qcv = [qcw, qcw2]
                c2v = [flat(X[s % 2])[:, 2 * TS:4 * TS].rearrange(
                           "p (a b) -> p a b", b=TS),
                       flat(X[s % 2])[:, 4 * TS:6 * TS].rearrange(
                           "p (a b) -> p a b", b=TS)]
                rb = repb[:, None, 0:n].broadcast_to([P, 2, n])

                def ap(k):
                    u, cch = (2 * k) // 8, (2 * k) % 8
                    p_ = k % 2
                    return (SILU[u][:, cch:cch + 2, 0:n],
                            hmv[p_][:, :, 0:n], qiv[p_][:, :, 0:n],
                            qcv[p_][:, :, 0:n], c2v[p_][:, :, 0:n])

                def s_m(k):
                    sp, hm, qi, qc, c2 = ap(k)
                    v.tensor_tensor(out=hm, in0=sp, in1=rb, op=ALU.mult)

                def s_i16(k):
                    sp, hm, qi, qc, c2 = ap(k)
                    v.tensor_scalar(out=qi, in0=hm, scalar1=-0.5,
                                    scalar2=None, op0=ALU.add)

                def s_c12(k):
                    sp, hm, qi, qc, c2 = ap(k)
                    v.scalar_tensor_tensor(out=qc, in0=hm, scalar=-0.5,
                                           in1=qi, op0=ALU.add, op1=ALU.is_gt)
                    v.scalar_tensor_tensor(out=c2, in0=hm, scalar=-1.5,
                                           in1=qi, op0=ALU.add, op1=ALU.is_gt)

                s_m(0)
                v.drain()
                s_i16(0)
                v.drain()
                s_c12(0)
                v.drain()
                for k in range(NP):
                    sp, hm, qi, qc, c2 = ap(k)
                    v.tensor_tensor(out=sp, in0=qi, in1=qc, op=ALU.add)
                    if k + 1 < NP:
                        s_m(k + 1)
                    v.drain()
                    v.scalar_tensor_tensor(out=qc, in0=hm, scalar=0.5,
                                           in1=qi, op0=ALU.add, op1=ALU.is_lt)
                    v.tensor_tensor(out=sp, in0=sp, in1=c2, op=ALU.add)
                    if k + 1 < NP:
                        s_i16(k + 1)
                    v.drain()
                    ii = v.tensor_tensor(out=sp, in0=sp, in1=qc,
                                         op=ALU.subtract)
                    einc("v", ii)
                    if k + 1 < NP:
                        s_c12(k + 1)
                    v.drain()
            v_xq2 = cnt["v"]

            fr_next = None
            if front_cb is not None:
                fr_next = front_cb({"t_f1e": t_f1e, "t_mxr2": t_mxr2 + 1,
                                    "v_s2": v_s2, "egm7": egm[7]})

            if last:
                @blk.vector
                def _(v):
                    v.wait_ge(sems["s"], s_ts)
                    v.wait_ge(sems["v"], v_xq2)
                    einc("v", v.reduce_max(out=smx[:, 0:1], in_=stail[:],
                                           axis=AX.X,
                                           apply_absolute_value=True))
                    v.drain()
                v_tm = cnt["v"]

                t_ttp = cnt["t"]

                @blk.tensor
                def _(t):
                    t.wait_ge(sems["v"], v_tm)
                    einc("t", t.transpose(pb[3][0:1, 0:P], smx[:, 0:1],
                                          idf[:]))

                @blk.vector
                def _(v):
                    v.wait_ge(sems["t"], t_ttp + 1)
                    v.reduce_max(out=mrow[0:1, 0:1], in_=pb[3][0:1, 0:P],
                                 axis=AX.X)
                    v.drain()
                    v.tensor_scalar(out=mrow[0:1, 0:1], in0=mrow[0:1, 0:1],
                                    scalar1=1e-5, scalar2=None, op0=ALU.max)
                    v.drain()
                    v.tensor_scalar(out=mrow[0:1, 1:2], in0=mrow[0:1, 0:1],
                                    scalar1=1.0 / 127, scalar2=None,
                                    op0=ALU.mult)
                    v.drain()
                    v.reciprocal(out=mrow[0:1, 2:3], in_=mrow[0:1, 0:1])
                    v.drain()
                    einc("v", v.tensor_scalar(out=mrow[0:1, 2:3],
                                              in0=mrow[0:1, 2:3],
                                              scalar1=127.0, scalar2=None,
                                              op0=ALU.mult))
                    v.drain()
                v_ts2 = cnt["v"]

                t_trp = cnt["t"]

                @blk.tensor
                def _(t):
                    t.wait_ge(sems["v"], v_ts2)
                    einc("t", t.matmul(pb[3][:, 0:2], ones_rf[:],
                                       mrow[0:1, 1:3], start=True, stop=True))

                @blk.scalar
                def _(sc):
                    sc.wait_ge(sems["t"], t_trp + 1)
                    einc("s", sc.copy(out=smx[:, 2:4], in_=pb[3][:, 0:2]))
                    sc.drain()
                s_trp = cnt["s"]

                @blk.vector
                def _(v):
                    v.wait_ge(sems["s"], s_trp)
                    v.tensor_scalar(out=hmt[0][:, 0:HC], in0=stail[:],
                                    scalar1=smx[:, 3:4],
                                    scalar2=None, op0=ALU.mult)
                    v.drain()
                    v.tensor_scalar(out=qia[:, 0:HC], in0=hmt[0][:, 0:HC],
                                    scalar1=-0.5, scalar2=None, op0=ALU.add)
                    v.drain()
                    v.scalar_tensor_tensor(out=qcb[:, 0:HC],
                                           in0=hmt[0][:, 0:HC], scalar=-0.5,
                                           in1=qia[:, 0:HC],
                                           op0=ALU.add, op1=ALU.is_gt)
                    v.scalar_tensor_tensor(out=qzb[:, 0:HC],
                                           in0=hmt[0][:, 0:HC], scalar=-1.5,
                                           in1=qia[:, 0:HC],
                                           op0=ALU.add, op1=ALU.is_gt)
                    v.scalar_tensor_tensor(out=rmx[:, 0:HC],
                                           in0=hmt[0][:, 0:HC], scalar=0.5,
                                           in1=qia[:, 0:HC],
                                           op0=ALU.add, op1=ALU.is_lt)
                    v.drain()
                    v.tensor_tensor(out=qtail[:], in0=qia[:, 0:HC],
                                    in1=qcb[:, 0:HC], op=ALU.add)
                    v.drain()
                    v.tensor_tensor(out=qtail[:], in0=qtail[:],
                                    in1=qzb[:, 0:HC], op=ALU.add)
                    v.drain()
                    einc("v", v.tensor_tensor(out=qtail[:], in0=qtail[:],
                                              in1=rmx[:, 0:HC],
                                              op=ALU.subtract))
                    v.drain()
                v_qt = cnt["v"]

            t_f2 = cnt["t"]
            v_f2_b = cnt["v"]
            ninc_t = 2 if last else 1
            ninc_v = 2 if last else 1

            @blk.tensor
            def _(t):
                for dc in range(CC):
                    t.wait_ge(sems["ldf2%d" % (dc % 2)],
                              ldf_b["2%d" % (dc % 2)] + 16 * (dc // 2 + 1))
                    t.wait_ge(sems["v"], v_xq2)
                    if last:
                        t.wait_ge(sems["v"], v_qt)
                    if dc >= 2:
                        t.wait_ge(sems["v"], v_f2_b + ninc_v * (dc - 2) + ninc_v)
                    for hc in range(HC):
                        ii = t.matmul(pb[dc % 2][:, 0:n],
                                      w2view(W2U[dc % 2])[:, hc, :],
                                      silu_ap(hc, n),
                                      start=(hc == 0), stop=(hc == HC - 1))
                    einc("t", ii)
                    if last:
                        for hc in range(HC):
                            ii = t.matmul(pb[2 + dc % 2][:, 0:1],
                                          w2view(W2U[dc % 2])[:, hc, :],
                                          qtail[:, hc:hc + 1],
                                          start=(hc == 0), stop=(hc == HC - 1))
                        einc("t", ii)
                    t.wait_ge(sems["t"], cnt["t"])
                    t.nop().then_inc(sems["wff"], 1)
                    cnt["wff"] += 1
            t_f2e = cnt["t"]

            @blk.vector
            def _(v):
                v.wait_ge(sems["sout"], sout_prev)
                for dc in range(CC):
                    v.wait_ge(sems["t"], t_f2 + ninc_t * dc + 1)
                    v.wait_ge(sems["x2w"], x2w_e)
                    v.scalar_tensor_tensor(out=OUTS[:, dc, 0:n],
                                           in0=pb[dc % 2][:, 0:n],
                                           scalar=m2v[:], in1=reps[3][:, 0:n],
                                           op0=ALU.mult, op1=ALU.mult)
                    v.drain()
                    ii = v.tensor_tensor(out=OUTS[:, dc, 0:n],
                                         in0=OUTS[:, dc, 0:n],
                                         in1=X2W[:, dc, 0:n], op=ALU.add)
                    einc("v", ii)
                    if last:
                        v.wait_ge(sems["t"], t_f2 + ninc_t * dc + 2)
                        v.tensor_scalar(out=OUTS[:, dc, n:n + 1],
                                        in0=pb[2 + dc % 2][:, 0:1],
                                        scalar1=m2v[:], scalar2=smx[:, 2:3],
                                        op0=ALU.mult, op1=ALU.mult)
                        v.drain()
                        ii = v.tensor_tensor(out=OUTS[:, dc, n:n + 1],
                                             in0=OUTS[:, dc, n:n + 1],
                                             in1=ttmp[:, dc:dc + 1],
                                             op=ALU.add)
                        einc("v", ii)
                        v.drain()
            v_outs = cnt["v"]

            @blk.scalar
            def _(sc):
                sc.wait_ge(sems["v"], v_outs)
                no = n + 1 if last else n
                dinc("sout", sc.dma_start(out=out[:, :, wlo:wlo + no],
                                          in_=OUTS[:, :, 0:no]))

            return ({"t_f1e": t_f1e, "t_f2e": t_f2e, "sout_e": cnt["sout"],
                     "v_cv": v_cv_all, "sgm7": sgm[7]}, fr_next)

    with block() as _mblk:
        def fill(nu):
            for _ in range(nu):
                if prep_work:
                    prep_work.pop(0)(_mblk)

        fr = attn_front(_mblk, 0, None, fill)
        fm = None
        for s in range(NSEG):
            am = attn_rest(_mblk, s, fr, fm, fill if s == 0 else None)
            cb = None
            if s + 1 < NSEG:
                cb = lambda ctx, s1=s + 1: attn_front(_mblk, s1, ctx)
            fm, fr = ffn_seg(_mblk, s, am, fill if s == 0 else None, cb, fm)
            if s == 0:
                assert not prep_work, f"prep leftover {len(prep_work)}"

    with block() as blk:
        total_out = cnt["sout"]

        @blk.sync
        def _(sy):
            sy.wait_ge(sems["sout"], total_out)

    es.close()
    return nc


def _arr(a, nchunk):
    a = np.ascontiguousarray(a, dtype=np.float32)
    return np.ascontiguousarray(a.reshape(nchunk, P, -1).transpose(1, 0, 2))


_NC_CACHE = {}


def kernel(L0_attn_x, L0_attn_W0, L0_attn_A_init, L0_attn_B_init, L0_attn_log_lr,
           L0_attn_Wq, L0_attn_Wk, L0_attn_Wv, L0_attn_Wo, L0_attn_Wg,
           L0_attn_ln_g, L0_attn_ln_b, L0_attn_ls_g, L0_ffn_w1, L0_ffn_conv_w,
           L0_ffn_conv_b, L0_ffn_w2):
    if "nc" not in _NC_CACHE:
        _NC_CACHE["nc"] = build_program()
    nc = _NC_CACHE["nc"]

    vec = lambda a: np.ascontiguousarray(
        np.asarray(a, np.float32).reshape(CC, P).T)
    common = {
        "wqt": _arr(np.asarray(L0_attn_Wq).T, CC),
        "wkt": _arr(np.asarray(L0_attn_Wk).T, CC),
        "wvt": _arr(np.asarray(L0_attn_Wv).T, CC),
        "wgt": _arr(np.asarray(L0_attn_Wg).T, CC),
        "wot": _arr(np.asarray(L0_attn_Wo).T, CC),
        "w0n": _arr(np.asarray(L0_attn_W0), CC),
        "w1t": _arr(np.asarray(L0_ffn_w1).T, CC),
        "w2t": _arr(np.asarray(L0_ffn_w2).T, HC),
        "a_t": np.ascontiguousarray(np.asarray(L0_attn_A_init, np.float32).T),
        "b_i": np.ascontiguousarray(np.asarray(L0_attn_B_init, np.float32)),
        "vlr": vec(L0_attn_log_lr),
        "vg": vec(L0_attn_ln_g),
        "vb": vec(L0_attn_ln_b),
        "vls": vec(L0_attn_ls_g),
        "vcw": np.ascontiguousarray(
            np.asarray(L0_ffn_conv_w, np.float32).reshape(HC, P, 3)
            .transpose(1, 0, 2)),
        "vcb": np.ascontiguousarray(np.asarray(L0_ffn_conv_b, np.float32).reshape(HC, P).T),
    }
    x = np.asarray(L0_attn_x, np.float32)
    B = x.shape[0]
    in_maps = []
    for b in range(B):
        m = dict(common)
        m["xin"] = _arr(x[b].T, CC)
        in_maps.append(m)
    res = run_bass_kernel_spmd(nc, in_maps, core_ids=list(range(B)))
    outs = []
    for b in range(B):
        arr = np.asarray(res.results[b]["out"])
        outs.append(arr.transpose(2, 1, 0).reshape(T, C))
    return np.stack(outs, axis=0).astype(np.float32)


# version 33
# speedup vs baseline: 1.0080x; 1.0080x over previous
import contextlib
import numpy as np
import concourse.bass as bass
import concourse.mybir as mybir
from concourse.bass_utils import run_bass_kernel_spmd

F32 = mybir.dt.float32
BF16 = mybir.dt.bfloat16
I32 = mybir.dt.int32
I16 = mybir.dt.int16
AF = mybir.ActivationFunctionType
ALU = mybir.AluOpType
AX = mybir.AxisListType

P, CC, HC, TS, NSEG = 128, 8, 32, 512, 4
T, C, H = TS * NSEG, P * CC, P * HC


def build_program():
    nc = bass.Bass("TRN2", target_bir_lowering=False, debug=False)

    xin = nc.dram_tensor("xin", [P, CC, T], F32, kind="ExternalInput").ap()
    wts = {n: nc.dram_tensor(n, [P, CC, C], F32, kind="ExternalInput").ap()
           for n in ("wkt", "wvt", "wgt", "w0n", "wqt", "wot")}
    w1t = nc.dram_tensor("w1t", [P, CC, H], F32, kind="ExternalInput").ap()
    w2t = nc.dram_tensor("w2t", [P, HC, C], F32, kind="ExternalInput").ap()
    a_t = nc.dram_tensor("a_t", [8, C], F32, kind="ExternalInput").ap()
    b_i = nc.dram_tensor("b_i", [8, C], F32, kind="ExternalInput").ap()
    pv = {n: nc.dram_tensor(n, [P, CC], F32, kind="ExternalInput").ap()
          for n in ("vlr", "vg", "vb", "vls")}
    vcw = nc.dram_tensor("vcw", [P, HC, 3], F32, kind="ExternalInput").ap()
    vcb = nc.dram_tensor("vcb", [P, HC], F32, kind="ExternalInput").ap()
    out = nc.dram_tensor("out", [P, CC, T], F32, kind="ExternalOutput").ap()

    at_s = nc.dram_tensor("at_s", [6, P, CC, C], BF16, kind="Internal").ap()
    w1_s = nc.dram_tensor("w1_s", [8, P, CC, TS], BF16, kind="Internal").ap()
    w2_s = nc.dram_tensor("w2_s", [CC, P, HC, P], BF16, kind="Internal").ap()
    x2_s = nc.dram_tensor("x2_s", [P, CC, T], F32, kind="Internal").ap()
    xq_s = nc.dram_tensor("xq_s", [P, CC, T], BF16, kind="Internal").ap()

    es = contextlib.ExitStack()
    sb = lambda name, shape, dt: es.enter_context(nc.sbuf_tensor(name, shape, dt))

    fu = [sb(f"fu{i}", [P, CC, 513], F32) for i in range(7)]
    bu = [sb(f"bu{i}", [P, CC, TS], BF16) for i in range(9)]
    pos = sb("pos", [P, T], BF16)
    qiw2 = sb("qiw2", [P, 2, TS], I16)
    qcw2 = sb("qcw2", [P, 2, TS], BF16)
    ones5 = sb("ones5", [P, TS], F32)
    idf = sb("idf", [P, P], F32)
    ones_c = sb("ones_c", [P, 1], BF16)
    ones_cf = sb("ones_cf", [P, 1], F32)
    ones_rf = sb("ones_rf", [1, P], F32)
    nh1 = sb("nh1", [1, 1], F32)
    ph1 = sb("ph1", [1, 1], F32)
    lrv = sb("lrv", [P, CC], F32)
    gvv = sb("gvv", [P, CC], F32)
    bvv = sb("bvv", [P, CC], F32)
    lsv = sb("lsv", [P, CC], F32)
    cwv = sb("cwv", [P, HC * 3], F32)
    cbv = sb("cbv", [P, HC], F32)
    m2v = sb("m2v", [P, 1], F32)
    m1v = sb("m1v", [P, 1], F32)
    sw1v = sb("sw1v", [P, 1], F32)
    sw2v = sb("sw2v", [P, 1], F32)
    wacc = sb("wacc", [P, 48], F32)
    wrow = sb("wrow", [1, 8], F32)
    reps = [sb(f"rep{i}", [P, TS], F32) for i in range(3)]
    repb = sb("repb", [P, TS], BF16)
    carry = sb("carry", [P, CC], F32)
    qiw = sb("qiw", [P, 2, TS], I16)
    qcw = sb("qcw", [P, 2, TS], BF16)
    qia = qiw[:, 0]
    qib = qiw[:, 1]
    qcb = qcw[:, 0]
    qzb = qcw[:, 1]
    smx = sb("smx", [P, 4], F32)
    tails = sb("tails", [P, HC, 2], F32)
    stail = sb("stail", [P, HC], F32)
    qtail = sb("qtail", [P, HC], BF16)
    ttmp = sb("ttmp", [P, CC], F32)

    pb = [es.enter_context(nc.psum_tensor(f"pb{i}", [P, TS], F32))
          for i in range(8)]

    SEMS = ("ldp", "ldx0", "ldx1", "lda0", "lda1", "ldq0", "ldq1",
            "ldfx", "ldf10", "ldf11", "ldf20", "ldf21", "spl", "sout",
            "sp0", "sp1", "ldq2", "ldq3", "spy",
            "x2w", "mrw", "t", "v", "s", "g", "wfa", "wff")
    sems, cnt = {}, {}
    for n in SEMS:
        sems[n] = es.enter_context(nc.semaphore("sem_" + n))
        cnt[n] = 0

    def bump(name, instr, step=1):
        instr.then_inc(sems[name], step)
        cnt[name] += step
        return cnt[name]

    dinc = lambda name, instr: bump(name, instr, 16)
    einc = lambda name, instr: bump(name, instr, 1)

    X = [fu[0], fu[1]]
    H1, KF, VS, Y0, WQS = fu[2], fu[3], fu[4], fu[5], fu[6]
    HB, KB, OB, GB, XQ, XS = bu[0], bu[1], bu[2], bu[3], bu[4], bu[5]
    WS = [bu[6], bu[7]]
    WQB = bu[8]
    SILU = [bu[0], bu[1], bu[2], bu[3]]
    W1U = [bu[6], bu[7]]
    W2U = [bu[5], bu[8]]
    X2W, OUTS = fu[3], fu[4]

    def flat(u):
        return u[:, :, :].rearrange("p a b -> p (a b)")

    def fl512(u):
        return u[:, :, 0:TS]

    def w2view(u):
        return flat(u).rearrange("p (a b) -> p a b", b=P)

    wqsf = WQS[:, :, :].rearrange("p a b -> p (a b)")
    cva = [wqsf[:, 0:TS], wqsf[:, TS:2 * TS]]
    cvb = [wqsf[:, 2 * TS:3 * TS], wqsf[:, 3 * TS:4 * TS]]
    hmt = [wqsf[:, 4 * TS:5 * TS], wqsf[:, 5 * TS:6 * TS]]
    rmx = wqsf[:, 6 * TS:7 * TS]
    reps.append(wqsf[:, 7 * TS:8 * TS])
    qc2 = wqsf[:, 3 * TS:4 * TS]
    qd2 = wqsf[:, TS:3 * TS]
    mrow = reps[2][0:4, 0:P]
    gtail = wqsf[:, 8 * TS:8 * TS + CC]
    def bc(r, n=TS):
        return r[:, None, 0:n].broadcast_to([P, CC, n])

    def silu_ap(hc, n=TS):
        return SILU[hc // 8][:, hc % 8, 0:n]

    blocks = []

    def block():
        b = nc.Block()
        blocks.append(b)
        return b

    with block() as blk:
        @blk.sync
        def _(sy):
            for src, dst in ((pv["vlr"], lrv), (pv["vg"], gvv), (pv["vb"], bvv),
                             (pv["vls"], lsv), (vcb, cbv)):
                dinc("ldp", sy.dma_start(out=dst[:], in_=src))
            dinc("ldp", sy.dma_start(
                out=cwv[:].rearrange("p (a b) -> p a b", b=3), in_=vcw))
            dinc("ldp", sy.dma_start(out=flat(WQS)[0:8, 0:C], in_=a_t))
            dinc("ldp", sy.dma_start(out=flat(WQS)[0:8, C:2 * C], in_=b_i))
            dinc("ldx0", sy.dma_start(out=fl512(X[0]), in_=xin[:, :, 0:TS]))
        ldp_all = cnt["ldp"]

        @blk.gpsimd
        def _(g):
            g.memset(ones5[:], 1.0)
            g.memset(ones_c[:], 1.0)
            g.memset(ones_cf[:], 1.0)
            g.memset(ones_rf[:], 1.0)
            g.memset(nh1[:], -0.5)
            g.memset(ph1[:], 0.5)
            g.memset(carry[:], 0.0)
            g.memset(gtail[:], 0.0)
            g.memset(tails[:, :, :].rearrange("p a b -> p (a b)"), 0.0)
            g.iota(idf[:], [[1, P]], base=0, channel_multiplier=-1,
                   allow_small_or_imprecise_dtypes=True)
            g.drain()
            g.tensor_scalar(out=idf[:], in0=idf[:], scalar1=0.0, scalar2=None,
                            op0=ALU.is_equal)
            g.drain()
            einc("g", g.iota(wqsf[0:1, 2 * C:2 * C + T], [[1, T]], base=1,
                             channel_multiplier=0,
                             allow_small_or_imprecise_dtypes=True))
        g_cst = cnt["g"]

        @blk.scalar
        def _(sc):
            sc.wait_ge(sems["g"], g_cst)
            sc.activation(out=wqsf[0:1, 2 * C:2 * C + T],
                          in_=wqsf[0:1, 2 * C:2 * C + T], func=AF.Ln)
            sc.drain()
            sc.wait_ge(sems["ldp"], ldp_all)
            einc("s", sc.activation(out=lrv[:], in_=lrv[:], func=AF.Exp))
            sc.drain()
        s_p0 = cnt["s"]

        @blk.vector
        def _(v):
            v.wait_ge(sems["s"], s_p0)
            v.tensor_scalar(out=lrv[:], in0=lrv[:], scalar1=1.0, scalar2=None,
                            op0=ALU.min)
            v.drain()
            v.tensor_scalar(out=wqsf[0:1, 2 * C:2 * C + T],
                            in0=wqsf[0:1, 2 * C:2 * C + T],
                            scalar1=0.1, scalar2=1.0, op0=ALU.mult, op1=ALU.add)
            v.drain()
            einc("v", v.reciprocal(out=wqsf[0:1, 2 * C:2 * C + T],
                                   in_=wqsf[0:1, 2 * C:2 * C + T]))
            v.drain()
        v_p0 = cnt["v"]

        t_pb = cnt["t"]
        s_pb = cnt["s"]

        @blk.tensor
        def _(t):
            t.wait_ge(sems["v"], v_p0)
            for j in range(NSEG):
                if j >= 2:
                    t.wait_ge(sems["s"], s_pb + j - 1)
                einc("t", t.matmul(pb[2 + j % 2][:], ones_rf[:],
                                   wqsf[0:1, 2 * C + j * TS:2 * C + (j + 1) * TS],
                                   start=True, stop=True))

        @blk.scalar
        def _(sc):
            for j in range(NSEG):
                sc.wait_ge(sems["t"], t_pb + j + 1)
                einc("s", sc.copy(out=pos[:, j * TS:(j + 1) * TS],
                                  in_=pb[2 + j % 2][:]))
                sc.drain()
    mark_lr = v_p0
    mark_posmm = cnt["t"]

    aorder = ["wkt", "wvt", "wgt", "w0n", "wqt", "wot"]
    with block() as blk:
        @blk.vector
        def _(v):
            v.wait_ge(sems["ldp"], ldp_all)
            v.tensor_copy(out=flat(WQB)[0:8, 0:C], in_=flat(WQS)[0:8, 0:C])
            v.drain()
            einc("v", v.tensor_copy(out=flat(WQB)[0:8, C:2 * C],
                                    in_=flat(WQS)[0:8, C:2 * C]))
            v.drain()
        v_ab = cnt["v"]

        lda_b = [cnt["lda0"], cnt["lda1"]]
        v_b = cnt["v"]
        t_b = cnt["t"]
        spl_b = cnt["spl"]
        nslab = 6 * CC

        @blk.sync
        def _(sy):
            sy.wait_ge(sems["v"], v_ab)
            sy.wait_ge(sems["t"], mark_posmm)
            for i in range(nslab):
                wi, ccx = divmod(i, CC)
                if i >= 2:
                    sy.wait_ge(sems["v"], v_b + i - 1)
                buf = 2 + i % 2
                dinc("lda%d" % (i % 2), sy.dma_start(
                    out=flat(WQS)[:, buf * C:(buf + 1) * C],
                    in_=wts[aorder[wi]][:, ccx, :]))

        @blk.tensor
        def _(t):
            t.wait_ge(sems["v"], v_ab)
            for k in range(16):
                ccx, hf = divmod(k, 2)
                if k >= 2:
                    t.wait_ge(sems["v"], v_b + 24 + (k - 2) // 2 + 1)
                einc("t", t.matmul(pb[4 + k % 2][:],
                                   flat(WQB)[0:8, ccx * P:(ccx + 1) * P],
                                   flat(WQB)[0:8, C + hf * TS:C + (hf + 1) * TS],
                                   start=True, stop=True))

        @blk.vector
        def _(v):
            for i in range(nslab):
                wi, ccx = divmod(i, CC)
                buf = 2 + i % 2
                tcol = buf * C
                v.wait_ge(sems["lda%d" % (i % 2)],
                          lda_b[i % 2] + 16 * (i // 2 + 1))
                if i >= 2:
                    v.wait_ge(sems["ldfx" if i % 2 == 0 else "spy"],
                              16 * (i // 2))
                if wi == 3:
                    for hf in range(2):
                        k = ccx * 2 + hf
                        v.wait_ge(sems["t"], t_b + k + 1)
                        ii = v.tensor_tensor(
                            out=flat(WQB)[:, tcol + hf * TS:
                                          tcol + (hf + 1) * TS],
                            in0=flat(WQS)[:, tcol + hf * TS:
                                          tcol + (hf + 1) * TS],
                            in1=pb[4 + k % 2][:], op=ALU.add)
                        if hf == 0:
                            v.drain()
                    einc("v", ii)
                else:
                    einc("v", v.tensor_copy(
                        out=flat(WQB)[:, tcol:tcol + C],
                        in_=flat(WQS)[:, tcol:tcol + C]))

        @blk.scalar
        def _(sc):
            for i in range(nslab):
                wi, ccx = divmod(i, CC)
                tcol = (2 + i % 2) * C
                sc.wait_ge(sems["v"], v_b + i + 1)
                dinc("ldfx" if i % 2 == 0 else "spy",
                     sc.dma_start(out=at_s[wi, :, ccx, :],
                                  in_=flat(WQB)[:, tcol:tcol + C]))
    p1_sp0 = cnt["ldfx"]
    p1_sp1 = cnt["spy"]

    x1f = flat(X[1])
    slab = [x1f[:, i * 1024:(i + 1) * 1024] for i in range(4)]
    wqbf = flat(WQB)
    terL = [wqbf[:, 0:1024], wqbf[:, 2048:3072]]
    terG = [wqbf[:, 1024:2048], wqbf[:, 3072:4096]]
    pm = {}
    pst = {"am": {}, "vm": {}, "ldw": {}, "spw": {},
           "ld": [0, 0, 0, 0], "sp": [0, 0], "next_ld": 0}

    def sem_idx(c):
        if 32 <= c < 48:
            return 2 * ((c - 32) % 2)
        return c % 4

    def chunk_src(c):
        if c < 32:
            q, ccx = divmod(c, 8)
            return w1t[:, ccx, q * 1024:(q + 1) * 1024]
        if c < 48:
            k = c - 32
            p = k % 2
            return (x1f[:, 2048 * p:2048 * p + 2048]
                    .rearrange("p (a b) -> p a b", b=C),
                    w2t[:, 2 * k:2 * k + 2, :])
        j = c - 48
        if j < 32:
            q, ccx = divmod(j, 8)
            return w1t[:, ccx, q * 1024:(q + 1) * 1024]
        return w2t[:, j - 32, :]

    def ld_cons(c):
        if c < 4:
            return None
        if c < 32:
            return ("s", pst["am"][c - 4])
        if c < 48:
            if c < 34:
                return ("s", pst["am"][29 + 2 * (c - 32)])
            return ("s", pst["am"][c - 2])
        j = c - 48
        if j < 4:
            return ("s", pst["am"][46 + j // 2])
        return ("v", pst["vm"][c - 4])

    def emit_loads(blk, upto):
        upto = min(upto, 112)
        if pst["next_ld"] >= upto:
            return

        @blk.gpsimd
        def _(g):
            while pst["next_ld"] < upto:
                c = pst["next_ld"]
                try:
                    w = ld_cons(c)
                except KeyError:
                    break
                if w is not None:
                    g.wait_ge(sems[w[0]], w[1])
                else:
                    g.wait_ge(sems["ldp"], ldp_all)
                si = sem_idx(c)
                if 32 <= c < 48:
                    dst, ssrc = chunk_src(c)
                    dinc("ldq%d" % si, g.dma_start(out=dst, in_=ssrc))
                else:
                    dinc("ldq%d" % si, g.dma_start(
                        out=slab[c % 4], in_=chunk_src(c)))
                pst["ld"][si] += 1
                pst["ldw"][c] = 16 * pst["ld"][si]
                pst["next_ld"] = c + 1

    def emit_spills(blk, cs):
        @blk.gpsimd
        def _(g):
            for c in cs:
                g.wait_ge(sems["v"], pst["vm"][c])
                j = c - 48
                if j < 32:
                    q, ccx = divmod(j, 8)
                    ii = g.dma_start(
                        out=w1_s[2 * q:2 * q + 2, :, ccx, :].rearrange(
                            "g p t -> p g t"),
                        in_=terG[c % 2].rearrange("p (g t) -> p g t", g=2))
                else:
                    ii = g.dma_start(
                        out=w2_s[:, :, j - 32, :].rearrange("d p c -> p d c"),
                        in_=terG[c % 2].rearrange("p (d c) -> p d c", d=CC))
                dinc("sp%d" % (c % 2), ii)
                pst["sp"][c % 2] += 1
                pst["spw"][c] = 16 * pst["sp"][c % 2]

    def mk_means_batch(c0, nb):
        def emit(blk):
            emit_loads(blk, c0 + nb)

            @blk.scalar
            def _(sc):
                for c in range(c0, c0 + nb):
                    sc.wait_ge(sems["ldq%d" % sem_idx(c)], pst["ldw"][c])
                    if c < 32:
                        ap, col = slab[c % 4], c
                    else:
                        k = c - 32
                        p = k % 2
                        ap, col = x1f[:, 2048 * p:2048 * p + 2048], 32 + k
                    einc("s", sc.activation(out=ap, in_=ap, func=AF.Abs,
                                            accum_out=wacc[:, col:col + 1]))
                    pst["am"][c] = cnt["s"]
            emit_loads(blk, c0 + 2 * nb)
        return emit

    def mk_sw_bundle():
        def emit(blk):
            @blk.tensor
            def _(t):
                t.wait_ge(sems["s"], pst["am"][47])
                einc("t", t.matmul(pb[6][0:1, 0:48], ones_cf[:], wacc[:],
                                   start=True, stop=True))
            t_ws = cnt["t"]

            @blk.vector
            def _(v):
                v.wait_ge(sems["t"], t_ws)
                v.reduce_sum(out=wrow[0:1, 0:1], in_=pb[6][0:1, 0:32],
                             axis=AX.X)
                v.reduce_sum(out=wrow[0:1, 1:2], in_=pb[6][0:1, 32:48],
                             axis=AX.X)
                v.drain()
                v.tensor_scalar(out=wrow[0:1, 2:4], in0=wrow[0:1, 0:2],
                                scalar1=1.0 / (C * H), scalar2=1e-5,
                                op0=ALU.mult, op1=ALU.max)
                v.drain()
                einc("v", v.reciprocal(out=wrow[0:1, 4:6], in_=wrow[0:1, 2:4]))
                v.drain()
            v_sw = cnt["v"]

            @blk.tensor
            def _(t):
                t.wait_ge(sems["v"], v_sw)
                einc("t", t.matmul(pb[6][:, 0:4], ones_rf[:], wrow[0:1, 2:6],
                                   start=True, stop=True))
            t_swr = cnt["t"]

            @blk.scalar
            def _(sc):
                sc.wait_ge(sems["t"], t_swr)
                sc.copy(out=m1v[:], in_=pb[6][:, 0:1])
                sc.copy(out=m2v[:], in_=pb[6][:, 1:2])
                sc.copy(out=sw1v[:], in_=pb[6][:, 2:3])
                sc.drain()
                einc("s", sc.copy(out=sw2v[:], in_=pb[6][:, 3:4]))
                sc.drain()
            pm["s_sw"] = cnt["s"]

            @blk.vector
            def _(v):
                v.wait_ge(sems["s"], pm["s_sw"])
                einc("v", v.tensor_scalar(out=cwv[:], in0=cwv[:],
                                          scalar1=m1v[:], scalar2=None,
                                          op0=ALU.mult))
                v.drain()
            pm["v_cwv"] = cnt["v"]
        return emit

    def mk_tern_batch(c0):
        def emit(blk):
            emit_loads(blk, c0 + 4)
            for half in range(2):
                @blk.vector
                def _(v, half=half):
                    if c0 == 48 and half == 0:
                        v.wait_ge(sems["s"], pm["s_sw"])
                    for c in (c0 + 2 * half, c0 + 2 * half + 1):
                        sw = sw1v if (c - 48) < 32 else sw2v
                        v.wait_ge(sems["ldq%d" % sem_idx(c)], pst["ldw"][c])
                        if c >= 50:
                            v.wait_ge(sems["sp%d" % (c % 2)],
                                      pst["spw"][c - 2])
                        v.tensor_scalar(out=terL[c % 2], in0=slab[c % 4],
                                        scalar1=sw[:], scalar2=-0.5,
                                        op0=ALU.mult, op1=ALU.is_lt)
                        v.tensor_scalar(out=terG[c % 2], in0=slab[c % 4],
                                        scalar1=sw[:], scalar2=0.5,
                                        op0=ALU.mult, op1=ALU.is_gt)
                    v.drain()
                    for c in (c0 + 2 * half, c0 + 2 * half + 1):
                        einc("v", v.tensor_tensor(out=terG[c % 2],
                                                  in0=terG[c % 2],
                                                  in1=terL[c % 2],
                                                  op=ALU.subtract))
                        pst["vm"][c] = cnt["v"]
                    v.drain()
                emit_spills(blk, (c0 + 2 * half, c0 + 2 * half + 1))
            emit_loads(blk, c0 + 8)
        return emit

    prep_work = ([mk_means_batch(c, 4) for c in range(0, 32, 4)]
                 + [mk_means_batch(c, 2) for c in range(32, 48, 2)]
                 + [mk_sw_bundle()]
                 + [mk_tern_batch(c) for c in range(48, 112, 4)])

    pm["sp0_w1"] = 16 * len([c for c in range(48, 80) if c % 2 == 0])
    pm["sp1_w1"] = 16 * len([c for c in range(48, 80) if c % 2 == 1])
    pm["sp0_all"] = 16 * len([c for c in range(48, 112) if c % 2 == 0])
    pm["sp1_all"] = 16 * len([c for c in range(48, 112) if c % 2 == 1])

    def attn_front(blk, s, ctx, fill=None):
        if True:
            xs = X[s % 2]
            XF = XQ
            pa, pc = (pb[2], pb[3]) if s == 0 else (pb[6], pb[7])
            R1a = wqsf[32:33, 0:TS]
            R1b = wqsf[32:33, TS:2 * TS]
            R1c = wqsf[32:33, 2 * TS:3 * TS]
            ST = lambda k: wqsf[0:1, k * TS:(k + 1) * TS]

            @blk.vector
            def _(v):
                v.wait_ge(sems["ldx%d" % (s % 2)], 16 * (s // 2 + 1))
                if ctx is not None:
                    v.wait_ge(sems["t"], ctx["t_f1e"])
                einc("v", v.tensor_copy(out=XF[:, :, :], in_=fl512(xs)))
            v_xbf = cnt["v"]
            if fill is not None:
                fill(2)

            @blk.tensor
            def _(t):
                t.wait_ge(sems["v"], v_xbf)
                if ctx is not None:
                    t.wait_ge(sems["s"], ctx["egm7"])
                for ccx in range(CC):
                    ii = t.matmul(pa[0:1, :], ones_c[:], XF[:, ccx, :],
                                  start=(ccx == 0), stop=(ccx == CC - 1))
                einc("t", ii)
            t_sA = cnt["t"]
            if fill is not None:
                fill(2)

            @blk.scalar
            def _(sc):
                sc.wait_ge(sems["t"], t_sA)
                einc("s", sc.activation(out=XF[:, :, :], in_=fl512(xs),
                                        func=AF.Square))
            s_sq = cnt["s"]

            @blk.tensor
            def _(t):
                t.wait_ge(sems["s"], s_sq)
                for ccx in range(CC):
                    ii = t.matmul(pc[0:1, :], ones_c[:], XF[:, ccx, :],
                                  start=(ccx == 0), stop=(ccx == CC - 1))
                einc("t", ii)
            t_sB = cnt["t"]
            if fill is not None:
                fill(2)

            @blk.vector
            def _(v):
                v.wait_ge(sems["t"], t_sB)
                v.tensor_scalar(out=ST(0), in0=pa[0:1, :], scalar1=1.0 / C,
                                scalar2=None, op0=ALU.mult)
                v.tensor_scalar(out=R1a, in0=pc[0:1, :], scalar1=1.0 / C,
                                scalar2=None, op0=ALU.mult)
                v.drain()
                v.tensor_tensor(out=R1b, in0=ST(0), in1=ST(0), op=ALU.mult)
                v.drain()
                v.tensor_tensor(out=R1a, in0=R1a, in1=R1b, op=ALU.subtract)
                v.drain()
                einc("v", v.tensor_scalar(out=R1b, in0=R1a, scalar1=1e-5,
                                          scalar2=None, op0=ALU.add))
                v.drain()
            v_u1 = cnt["v"]
            if fill is not None:
                fill(2)

            @blk.scalar
            def _(sc):
                sc.wait_ge(sems["v"], v_u1)
                sc.activation(out=ST(3), in_=R1b, func=AF.Ln)
                sc.drain()
                einc("s", sc.activation(out=ST(1), in_=ST(3), func=AF.Exp,
                                        scale=nh1[:]))
            s_r1 = cnt["s"]

            @blk.vector
            def _(v):
                v.reciprocal(out=R1c, in_=R1b)
                v.drain()
                v.tensor_tensor(out=R1a, in0=R1a, in1=R1c, op=ALU.mult)
                v.drain()
                einc("v", v.tensor_scalar(out=R1a, in0=R1a, scalar1=1e-5,
                                          scalar2=None, op0=ALU.add))
                v.drain()
            v_u2 = cnt["v"]
            if fill is not None:
                fill(2)

            @blk.scalar
            def _(sc):
                sc.wait_ge(sems["v"], v_u2)
                sc.wait_ge(sems["s"], s_r1)
                sc.activation(out=R1b, in_=R1a, func=AF.Ln)
                sc.drain()
                sc.activation(out=ST(2), in_=R1b, func=AF.Exp, scale=nh1[:])
                einc("s", sc.activation(out=R1c, in_=R1b, func=AF.Exp,
                                        scale=ph1[:]))
            s_sv = cnt["s"]

            @blk.vector
            def _(v):
                v.wait_ge(sems["s"], s_sv)
                v.tensor_tensor(out=ST(1), in0=ST(1), in1=ST(2), op=ALU.mult)
                v.drain()
                einc("v", v.tensor_tensor(out=ST(0), in0=ST(0), in1=ST(1),
                                          op=ALU.mult))
            v_ab2 = cnt["v"]
            if fill is not None:
                fill(1)

            t_rp = cnt["t"]
            s_rp = cnt["s"]

            @blk.tensor
            def _(t):
                t.wait_ge(sems["v"], v_ab2)
                for j, srcap in enumerate((ST(1), ST(0), R1c)):
                    if j >= 2:
                        t.wait_ge(sems["s"], s_rp + j - 1)
                    lhs = ones_rf[:] if j < 2 else ones5[32:33, 0:P]
                    einc("t", t.matmul((pa if j % 2 == 0 else pc)[:],
                                       lhs, srcap,
                                       start=True, stop=True))

            @blk.scalar
            def _(sc):
                if ctx is not None:
                    sc.wait_ge(sems["t"], ctx["t_mxr2"])
                    sc.wait_ge(sems["v"], ctx["v_s2"])
                for j in range(3):
                    sc.wait_ge(sems["t"], t_rp + j + 1)
                    einc("s", sc.copy(out=reps[j][:],
                                      in_=(pa if j % 2 == 0 else pc)[:]))
            s_reps = cnt["s"]
            if fill is not None:
                fill(1)

            @blk.vector
            def _(v):
                v.wait_ge(sems["s"], s_reps)
                v.tensor_tensor(out=H1[:, :, 0:TS], in0=xs[:, :, 0:TS],
                                in1=bc(reps[0]), op=ALU.mult)
                v.drain()
                einc("v", v.tensor_tensor(out=H1[:, :, 0:TS],
                                          in0=H1[:, :, 0:TS],
                                          in1=bc(reps[1]), op=ALU.subtract))
                v.drain()
            v_h1f = cnt["v"]
            return {"t_sB": t_sB, "s_reps": s_reps, "v_h1f": v_h1f}

    def attn_rest(blk, s, fr, fm, fill=None):
        if True:
            xs = X[s % 2]
            lo = s * TS
            lda_b = [cnt["lda0"], cnt["lda1"]]
            wfa_b = cnt["wfa"]

            @blk.sync
            def _(sy):
                sy.wait_ge(sems["ldfx"], p1_sp0)
                sy.wait_ge(sems["spy"], p1_sp1)
                if fm is not None:
                    sy.wait_ge(sems["t"], fm["t_f1e"])
                for j in range(12):
                    wi, hf = divmod(j, 2)
                    if wfa_b + j - 1 > 0:
                        sy.wait_ge(sems["wfa"], max(0, wfa_b + j - 1))
                    dinc("lda%d" % (j % 2), sy.dma_start(
                        out=WS[j % 2][:, :, :],
                        in_=at_s[wi, :, :, hf * TS:(hf + 1) * TS]))

            ST = lambda k: wqsf[0:1, k * TS:(k + 1) * TS]

            @blk.vector
            def _(v):
                v.wait_ge(sems["s"], fr["s_reps"])
                for ccx in range(CC):
                    einc("v", v.tensor_scalar(out=HB[:, ccx, :],
                                              in0=H1[:, ccx, 0:TS],
                                              scalar1=gvv[:, ccx:ccx + 1],
                                              scalar2=bvv[:, ccx:ccx + 1],
                                              op0=ALU.mult, op1=ALU.add))
                v.drain()
                v.tensor_tensor(out=H1[:, :, 0:TS], in0=H1[:, :, 0:TS],
                                in1=bc(reps[2]), op=ALU.mult)
                v.drain()
                einc("v", v.tensor_copy(out=OB[:, :, :], in_=fl512(H1)))
            v_h1 = cnt["v"]
            if fill is not None:
                fill(1)

            t_kvg = cnt["t"]
            s_kvg_b = cnt["s"]

            @blk.tensor
            def _(t):
                for j in range(6):
                    wi, hf = divmod(j, 2)
                    t.wait_ge(sems["lda%d" % (j % 2)],
                              lda_b[j % 2] + 16 * (j // 2 + 1))
                    t.wait_ge(sems["v"], v_h1 - 1 if j < 4 else v_h1)
                    for dcl in range(4):
                        gidx = j * 4 + dcl
                        if gidx >= 4:
                            t.wait_ge(sems["s"], s_kvg_b + gidx - 3)
                        mvt = HB if wi < 2 else OB
                        for ccx in range(CC):
                            ii = t.matmul(pb[gidx % 4][:],
                                          WS[j % 2][:, ccx, dcl * P:(dcl + 1) * P],
                                          mvt[:, ccx, :], start=(ccx == 0),
                                          stop=(ccx == CC - 1))
                        einc("t", ii)

            @blk.scalar
            def _(sc):
                if fm is not None:
                    sc.wait_ge(sems["sout"], fm["sout_e"])
                for j in range(6):
                    wi, hf = divmod(j, 2)
                    for dcl in range(4):
                        dc = hf * 4 + dcl
                        gidx = j * 4 + dcl
                        sc.wait_ge(sems["t"], t_kvg + gidx + 1)
                        if wi == 0:
                            ii = sc.copy(out=KF[:, dc, 1:513],
                                         in_=pb[gidx % 4][:])
                        elif wi == 1:
                            ii = sc.copy(out=VS[:, dc, 0:TS],
                                         in_=pb[gidx % 4][:])
                        else:
                            ii = sc.activation(out=GB[:, dc, :],
                                               in_=pb[gidx % 4][:],
                                               func=AF.Sigmoid)
                        einc("s", ii)
                        if dcl == 3:
                            sc.drain().then_inc(sems["wfa"], 1)
                            cnt["wfa"] += 1
            s_kvg = cnt["s"]
            if fill is not None:
                fill(1)

            @blk.vector
            def _(v):
                v.wait_ge(sems["s"], s_kvg_b + 8)
                einc("v", v.tensor_copy(out=KB[:, :, :], in_=KF[:, :, 1:513]))
            v_kb = cnt["v"]
            if fill is not None:
                fill(1)

            t_y = cnt["t"]
            s_y_b = cnt["s"]

            ybk = (lambda gidx: pb[4 + gidx % 4]) if s > 0 else \
                (lambda gidx: pb[gidx % 4])

            @blk.tensor
            def _(t):
                for j in range(6, 8):
                    t.wait_ge(sems["lda%d" % (j % 2)],
                              lda_b[j % 2] + 16 * (j // 2 + 1))
                    t.wait_ge(sems["v"], v_kb)
                    for dcl in range(4):
                        gidx = (j - 6) * 4 + dcl
                        if gidx >= 4:
                            t.wait_ge(sems["s"], s_y_b + gidx - 3)
                        elif s == 0:
                            t.wait_ge(sems["s"], s_kvg)
                        for ccx in range(CC):
                            ii = t.matmul(ybk(gidx)[:],
                                          WS[j % 2][:, ccx, dcl * P:(dcl + 1) * P],
                                          KB[:, ccx, :], start=(ccx == 0),
                                          stop=(ccx == CC - 1))
                        einc("t", ii)

            @blk.scalar
            def _(sc):
                for gidx in range(8):
                    sc.wait_ge(sems["t"], t_y + gidx + 1)
                    ii = sc.copy(out=Y0[:, gidx, 0:TS], in_=ybk(gidx)[:])
                    einc("s", ii)
                    if gidx % 4 == 3:
                        sc.drain().then_inc(sems["wfa"], 1)
                        cnt["wfa"] += 1
            s_y0 = cnt["s"]
            if fill is not None:
                fill(1)

            @blk.vector
            def _(v):
                v.wait_ge(sems["s"], s_y_b + 4)
                v.tensor_tensor(out=VS[:, 0:4, 0:TS], in0=Y0[:, 0:4, 0:TS],
                                in1=VS[:, 0:4, 0:TS], op=ALU.subtract)
                v.drain()
                v.tensor_tensor(out=KF[:, 0:4, 1:513], in0=VS[:, 0:4, 0:TS],
                                in1=KF[:, 0:4, 1:513], op=ALU.mult)
                v.wait_ge(sems["s"], s_y0)
                v.tensor_tensor(out=VS[:, 4:8, 0:TS], in0=Y0[:, 4:8, 0:TS],
                                in1=VS[:, 4:8, 0:TS], op=ALU.subtract)
                v.drain()
                v.tensor_tensor(out=KF[:, 4:8, 1:513],
                                in0=VS[:, 4:8, 0:TS],
                                in1=KF[:, 4:8, 1:513],
                                op=ALU.mult)
                v.tensor_copy(out=KF[:, :, 0], in_=gtail[:])
                v.drain()
                for ccx in range(CC):
                    v.tensor_tensor_scan(out=VS[:, ccx, 0:TS], data0=ones5[:],
                                         data1=KF[:, ccx, 0:TS],
                                         initial=carry[:, ccx:ccx + 1],
                                         op0=ALU.mult, op1=ALU.add)
                v.drain()
                v.tensor_copy(out=carry[:], in_=VS[:, :, 511])
                einc("v", v.tensor_copy(out=gtail[:], in_=KF[:, :, 512]))
            v_gs = cnt["v"]
            if fill is not None:
                fill(1)

            t_q = cnt["t"]
            v_q_b = cnt["v"]

            @blk.tensor
            def _(t):
                for j in range(8, 10):
                    t.wait_ge(sems["lda%d" % (j % 2)],
                              lda_b[j % 2] + 16 * (j // 2 + 1))
                    t.wait_ge(sems["s"], s_y0 if s == 0 else s_kvg)
                    t.wait_ge(sems["v"], v_h1)
                    for dcl in range(4):
                        gidx = (j - 8) * 4 + dcl
                        if gidx >= 4:
                            t.wait_ge(sems["v"], v_q_b + gidx - 3)
                        for ccx in range(CC):
                            ii = t.matmul(pb[gidx % 4][:],
                                          WS[j % 2][:, ccx, dcl * P:(dcl + 1) * P],
                                          HB[:, ccx, :], start=(ccx == 0),
                                          stop=(ccx == CC - 1))
                        einc("t", ii)

            @blk.vector
            def _(v):
                v.wait_ge(sems["v"], mark_lr)
                v.wait_ge(sems["v"], v_gs)
                for gidx in range(8):
                    v.wait_ge(sems["t"], t_q + gidx + 1)
                    ii = v.scalar_tensor_tensor(
                        out=KF[:, gidx, 1:513], in0=pb[gidx % 4][:],
                        scalar=lrv[:, gidx:gidx + 1], in1=VS[:, gidx, 0:TS],
                        op0=ALU.mult, op1=ALU.mult)
                    einc("v", ii)
                    if gidx % 4 == 3:
                        v.drain().then_inc(sems["wfa"], 1)
                        cnt["wfa"] += 1
                v.drain()
                v.tensor_tensor(out=KF[:, :, 1:513], in0=KF[:, :, 1:513],
                                in1=pos[:, None, lo:lo + TS]
                                .broadcast_to([P, CC, TS]), op=ALU.mult)
                v.drain()
                v.tensor_tensor(out=fl512(Y0), in0=fl512(Y0),
                                in1=KF[:, :, 1:513],
                                op=ALU.subtract)
                v.drain()
                einc("v", v.tensor_tensor(out=OB[:, :, :], in0=fl512(Y0),
                                          in1=GB[:, :, :], op=ALU.mult))
            v_o1 = cnt["v"]
            if fill is not None:
                fill(1)

            t_o = cnt["t"]
            v_wo_b = cnt["v"]

            @blk.tensor
            def _(t):
                for j in range(10, 12):
                    t.wait_ge(sems["lda%d" % (j % 2)],
                              lda_b[j % 2] + 16 * (j // 2 + 1))
                    t.wait_ge(sems["v"], v_o1)
                    for dcl in range(4):
                        gidx = (j - 10) * 4 + dcl
                        if gidx >= 4:
                            t.wait_ge(sems["v"], v_wo_b + gidx - 3)
                        for ccx in range(CC):
                            ii = t.matmul(pb[gidx % 4][:],
                                          WS[j % 2][:, ccx, dcl * P:(dcl + 1) * P],
                                          OB[:, ccx, :], start=(ccx == 0),
                                          stop=(ccx == CC - 1))
                        einc("t", ii)

            @blk.vector
            def _(v):
                for gidx in range(8):
                    v.wait_ge(sems["t"], t_o + gidx + 1)
                    ii = v.tensor_scalar(out=VS[:, gidx, 0:TS],
                                         in0=pb[gidx % 4][:],
                                         scalar1=lsv[:, gidx:gidx + 1],
                                         scalar2=None, op0=ALU.mult)
                    einc("v", ii)
                    if gidx % 4 == 3:
                        v.drain().then_inc(sems["wfa"], 1)
                        cnt["wfa"] += 1
                v.drain()
                v.tensor_tensor(out=fl512(VS), in0=fl512(VS), in1=fl512(H1),
                                op=ALU.add)
                v.drain()
                einc("v", v.tensor_tensor(out=fl512(xs), in0=fl512(VS),
                                          in1=fl512(xs), op=ALU.add))
                v.drain()
            v_x2 = cnt["v"]
            if fill is not None:
                fill(1)

            @blk.scalar
            def _(sc):
                sc.wait_ge(sems["v"], v_x2)
                sc.wait_ge(sems["spl"], cnt["spl"])
                dinc("spl", sc.dma_start(out=x2_s[:, :, lo:lo + TS],
                                         in_=fl512(xs)))
            spl_x2 = cnt["spl"]

            @blk.vector
            def _(v):
                v.wait_ge(sems["v"], v_x2)
                einc("v", v.tensor_copy(out=XS[:, :, :], in_=fl512(xs)))
            v_x2b = cnt["v"]
            if fill is not None:
                fill(1)

            @blk.tensor
            def _(t):
                t.wait_ge(sems["v"], v_x2b)
                for ccx in range(CC):
                    ii = t.matmul(pb[2][0:1, :], ones_c[:], XS[:, ccx, :],
                                  start=(ccx == 0), stop=(ccx == CC - 1))
                einc("t", ii)
            t_3A = cnt["t"]

            @blk.scalar
            def _(sc):
                sc.wait_ge(sems["v"], v_x2b)
                einc("s", sc.activation(out=KB[:, :, :], in_=fl512(xs),
                                        func=AF.Square))
            s_3sq = cnt["s"]

            @blk.tensor
            def _(t):
                t.wait_ge(sems["s"], s_3sq)
                for ccx in range(CC):
                    ii = t.matmul(pb[3][0:1, :], ones_c[:], KB[:, ccx, :],
                                  start=(ccx == 0), stop=(ccx == CC - 1))
                einc("t", ii)
            t_3B = cnt["t"]

            @blk.vector
            def _(v):
                v.wait_ge(sems["t"], t_3B)
                v.tensor_scalar(out=ST(0), in0=pb[2][0:1, :], scalar1=1.0 / C,
                                scalar2=None, op0=ALU.mult)
                v.tensor_scalar(out=ST(1), in0=pb[3][0:1, :], scalar1=1.0 / C,
                                scalar2=None, op0=ALU.mult)
                v.drain()
                v.tensor_tensor(out=ST(3), in0=ST(0), in1=ST(0), op=ALU.mult)
                v.drain()
                v.tensor_tensor(out=ST(1), in0=ST(1), in1=ST(3),
                                op=ALU.subtract)
                v.drain()
                einc("v", v.tensor_scalar(out=ST(2), in0=ST(1), scalar1=1e-5,
                                          scalar2=None, op0=ALU.add))
                v.drain()
            v_3u = cnt["v"]
            if fill is not None:
                fill(1)

            @blk.scalar
            def _(sc):
                sc.wait_ge(sems["v"], v_3u)
                sc.activation(out=ST(3), in_=ST(2), func=AF.Ln)
                sc.drain()
                einc("s", sc.activation(out=ST(1), in_=ST(3), func=AF.Exp,
                                        scale=nh1[:]))
            s_r3 = cnt["s"]

            t_3r = cnt["t"]

            @blk.tensor
            def _(t):
                t.wait_ge(sems["s"], s_r3)
                einc("t", t.matmul(pb[2][:], ones_rf[:], ST(0), start=True,
                                   stop=True))
                einc("t", t.matmul(pb[3][:], ones_rf[:], ST(1), start=True,
                                   stop=True))

            @blk.scalar
            def _(sc):
                sc.wait_ge(sems["t"], t_3r + 2)
                sc.copy(out=reps[0][:], in_=pb[2][:])
                einc("s", sc.copy(out=reps[1][:], in_=pb[3][:]))
            s_3rep = cnt["s"]
            if fill is not None:
                fill(1)

            @blk.vector
            def _(v):
                v.wait_ge(sems["s"], s_3rep)
                v.tensor_tensor(out=H1[:, :, 0:TS], in0=xs[:, :, 0:TS],
                                in1=bc(reps[0]), op=ALU.subtract)
                v.drain()
                v.scalar_tensor_tensor(out=Y0[:, :, 0:TS], in0=H1[:, :, 0:TS],
                                       scalar=-1.0, in1=H1[:, :, 0:TS],
                                       op0=ALU.mult, op1=ALU.max)
                v.drain()
                v.tensor_tensor(out=Y0[:, 0:4, 0:TS], in0=Y0[:, 0:4, 0:TS],
                                in1=Y0[:, 4:8, 0:TS], op=ALU.max)
                v.drain()
                v.tensor_tensor(out=Y0[:, 0:2, 0:TS], in0=Y0[:, 0:2, 0:TS],
                                in1=Y0[:, 2:4, 0:TS], op=ALU.max)
                v.drain()
                einc("v", v.tensor_tensor(out=rmx[:], in0=Y0[:, 0, 0:TS],
                                          in1=Y0[:, 1, 0:TS], op=ALU.max))
            v_rmx = cnt["v"]
            if fill is not None:
                fill(1)

            t_tp = cnt["t"]
            v_tp_b = cnt["v"]
            tp2bank = (pb[3], pb[4], pb[5], pb[2])

            @blk.tensor
            def _(t):
                t.wait_ge(sems["v"], v_rmx)
                for j in range(4):
                    if j >= 1:
                        t.wait_ge(sems["v"], v_tp_b + j)
                        einc("t", t.transpose(
                            tp2bank[j - 1][0:1, (j - 1) * P:j * P],
                            smx[:, j - 1:j], idf[:]))
                    einc("t", t.transpose(pb[2][:, 0:P],
                                          rmx[:, j * P:(j + 1) * P], idf[:]))
                t.wait_ge(sems["v"], v_tp_b + 4)
                einc("t", t.transpose(tp2bank[3][0:1, 3 * P:4 * P],
                                      smx[:, 3:4], idf[:]))

            @blk.vector
            def _(v):
                for j in range(4):
                    v.wait_ge(sems["t"], t_tp + 2 * j + 1)
                    ii = v.reduce_max(out=smx[:, j:j + 1], in_=pb[2][:, 0:P],
                                      axis=AX.X)
                    einc("v", ii)
            v_smx = cnt["v"]
            t_tp8 = cnt["t"]

            @blk.scalar
            def _(sc):
                sc.wait_ge(sems["t"], t_tp8)
                for j in range(4):
                    ii = sc.copy(
                        out=wqsf[0:1, 3 * TS + j * P:3 * TS + (j + 1) * P],
                        in_=tp2bank[j][0:1, j * P:(j + 1) * P])
                einc("s", ii)
                sc.drain()
            s_mrow = cnt["s"]

            t_mxr = cnt["t"]

            @blk.tensor
            def _(t):
                t.wait_ge(sems["s"], s_mrow)
                einc("t", t.matmul(pb[2][:], ones_rf[:],
                                   wqsf[0:1, 3 * TS:4 * TS],
                                   start=True, stop=True))

            @blk.scalar
            def _(sc):
                sc.wait_ge(sems["t"], t_mxr + 1)
                einc("s", sc.copy(out=reps[2][:], in_=pb[2][:]))
                sc.drain()
            s_mx = cnt["s"]
            if fill is not None:
                fill(1)

            @blk.vector
            def _(v):
                v.wait_ge(sems["s"], s_mx)
                v.tensor_tensor(out=reps[2][:], in0=reps[2][:], in1=reps[1][:],
                                op=ALU.mult)
                v.drain()
                v.tensor_scalar(out=reps[2][:], in0=reps[2][:], scalar1=1e-5,
                                scalar2=None, op0=ALU.max)
                v.drain()
                v.tensor_scalar(out=qc2[:], in0=reps[2][:], scalar1=1.0 / 127,
                                scalar2=None, op0=ALU.mult)
                v.reciprocal(out=reps[3][:], in_=reps[2][:])
                v.drain()
                v.scalar_tensor_tensor(out=reps[3][:], in0=reps[1][:],
                                       scalar=127.0, in1=reps[3][:],
                                       op0=ALU.mult, op1=ALU.mult)
                v.drain()
                einc("v", v.tensor_tensor(out=H1[:, :, 0:TS],
                                          in0=H1[:, :, 0:TS],
                                          in1=bc(reps[3]), op=ALU.mult))
                v.drain()
            v_qq = cnt["v"]
            if fill is not None:
                fill(1)

            v_q1_b = cnt["v"]

            @blk.vector
            def _(v):
                v.wait_ge(sems["t"], fr["t_sB"])
                v.wait_ge(sems["spl"], spl_x2)
                NP = CC // 2
                qiv = [qiw, qiw2]
                qcv = [qcw, qcw2]
                c2v = [qd2.rearrange("p (a b) -> p a b", b=TS),
                       flat(xs)[:, 0:2 * TS].rearrange(
                           "p (a b) -> p a b", b=TS)]

                def ap(k):
                    p_ = k % 2
                    return (H1[:, 2 * k:2 * k + 2, 0:TS],
                            XQ[:, 2 * k:2 * k + 2, :],
                            qiv[p_][:, :, :], qcv[p_][:, :, :], c2v[p_])

                def s_i16(k):
                    hp, xp, qi, qc, c2 = ap(k)
                    v.tensor_scalar(out=qi, in0=hp, scalar1=-0.5,
                                    scalar2=None, op0=ALU.add)

                def s_c12(k):
                    hp, xp, qi, qc, c2 = ap(k)
                    v.scalar_tensor_tensor(out=qc, in0=hp, scalar=-0.5,
                                           in1=qi, op0=ALU.add, op1=ALU.is_gt)
                    v.scalar_tensor_tensor(out=c2, in0=hp, scalar=-1.5,
                                           in1=qi, op0=ALU.add, op1=ALU.is_gt)

                s_i16(0)
                v.drain()
                s_c12(0)
                v.drain()
                for k in range(NP):
                    hp, xp, qi, qc, c2 = ap(k)
                    v.tensor_tensor(out=xp, in0=qi, in1=qc, op=ALU.add)
                    if k + 1 < NP:
                        s_i16(k + 1)
                    v.drain()
                    v.scalar_tensor_tensor(out=qc, in0=hp, scalar=0.5,
                                           in1=qi, op0=ALU.add, op1=ALU.is_lt)
                    v.tensor_tensor(out=xp, in0=xp, in1=c2, op=ALU.add)
                    if k + 1 < NP:
                        s_c12(k + 1)
                    v.drain()
                    ii = v.tensor_tensor(out=xp, in0=xp, in1=qc,
                                         op=ALU.subtract)
                    einc("v", ii)
                    v.drain()
            v_xq1 = cnt["v"]
            return {"v_xq1": v_xq1, "v_q1_b": v_q1_b,
                    "wfa_end": cnt["wfa"]}

    def ffn_seg(blk, s, am, fill=None, front_cb=None, pfm=None):
        if True:
            lo = s * TS
            n = TS if s > 0 else TS - 1
            wlo = lo - 1 if s > 0 else 0
            j0 = 0 if s > 0 else 1
            last = (s == NSEG - 1)
            spl_done = cnt["spl"]
            sout_prev = cnt["sout"]
            ldf_b = {"10": cnt["ldf10"], "11": cnt["ldf11"],
                     "20": cnt["ldf20"], "21": cnt["ldf21"]}
            wff_b = cnt["wff"]

            @blk.sync
            def _(sy):
                sy.wait_ge(sems["spl"], spl_done)
                sy.wait_ge(sems["wfa"], am["wfa_end"])
                if s >= 1 and s + 1 < NSEG:
                    sy.wait_ge(sems["v"], pfm["v_cv"])
                    sy.wait_ge(sems["s"], pfm["sgm7"])
                    dinc("ldx%d" % ((s + 1) % 2), sy.dma_start(
                        out=fl512(X[(s + 1) % 2]),
                        in_=xin[:, :, lo + TS:lo + 2 * TS]))
                if s == 0:
                    sy.wait_ge(sems["sp0"], pm["sp0_w1"])
                    sy.wait_ge(sems["sp1"], pm["sp1_w1"])
                for gi in range(8):
                    if gi >= 2:
                        sy.wait_ge(sems["wff"], wff_b + gi - 1)
                    elif wff_b + gi - 9 > 0:
                        sy.wait_ge(sems["wff"], wff_b + gi - 9)
                    dinc("ldf1%d" % (gi % 2), sy.dma_start(
                        out=W1U[gi % 2][:, :, :], in_=w1_s[gi, :, :, :]))

            def emit_ldf2():
                @blk.sync
                def _(sy):
                    if s == 0:
                        sy.wait_ge(sems["sp0"], pm["sp0_all"])
                        sy.wait_ge(sems["sp1"], pm["sp1_all"])
                    for dc in range(CC):
                        if dc < 2:
                            sy.wait_ge(sems["wff"], wff_b + 8)
                        else:
                            sy.wait_ge(sems["wff"], wff_b + 7 + dc)
                        dinc("ldf2%d" % (dc % 2), sy.dma_start(
                            out=w2view(W2U[dc % 2]), in_=w2_s[dc, :, :, :]))

            x2w_b = cnt["x2w"]

            @blk.scalar
            def _(sc):
                sc.wait_ge(sems["spl"], spl_done)
                dinc("x2w", sc.dma_start(out=X2W[:, :, 0:n],
                                         in_=x2_s[:, :, wlo:wlo + n]))
                if last:
                    dinc("x2w", sc.dma_start(out=ttmp[:],
                                             in_=x2_s[:, :, T - 1],
                                             allow_slow_non_contiguous=True))
            x2w_e = cnt["x2w"]

            stx = [flat(X[s % 2])[:, 516 * i:516 * i + 516] for i in range(7)]
            stx.append(wqsf[:, 0:516])
            tg = lambda g: [H1[:, (g % 2) * 4 + j, 0:TS] for j in range(4)]
            tbv = [VS[:, j, 0:TS] for j in range(4)]

            t_f1 = cnt["t"]
            s_f_b = cnt["s"]
            egm, sgm = {}, {}
            k = s_f_b
            for g in range(8):
                k += 1
                egm[g] = k
                if g >= 1:
                    k += 1
                    sgm[g - 1] = k
            k += 1
            sgm[7] = k
            v_cg_b = cnt["v"]

            fbk = lambda g, hc: pb[(4 if g % 2 == 0 else 0) + hc % 4]

            @blk.tensor
            def _(t):
                for gi in range(8):
                    if gi == 0 and s == 0:
                        t.wait_ge(sems["s"], pm["s_sw"])
                    if gi == 1:
                        t.wait_ge(sems["v"], am["v_xq1"])
                    t.wait_ge(sems["ldf1%d" % (gi % 2)],
                              ldf_b["1%d" % (gi % 2)] + 16 * (gi // 2 + 1))
                    if gi >= 2:
                        t.wait_ge(sems["s"], egm[gi - 2])
                    for hi in range(4):
                        hc = gi * 4 + hi
                        for ccx in range(CC):
                            if gi == 0 and hi == 0:
                                t.wait_ge(sems["v"],
                                          am["v_q1_b"] + ccx // 2 + 1)
                            ii = t.matmul(fbk(gi, hc)[:],
                                          W1U[gi % 2][:, ccx, hi * P:(hi + 1) * P],
                                          XQ[:, ccx, :], start=(ccx == 0),
                                          stop=(ccx == CC - 1))
                        einc("t", ii)
                    t.wait_ge(sems["t"], cnt["t"])
                    t.nop().then_inc(sems["wff"], 1)
                    cnt["wff"] += 1
            t_f1e = cnt["t"]

            cg_mark = {}
            scl_mark = {}

            def emit_scale(g):
                @blk.vector
                def _(v, g=g):
                    v.wait_ge(sems["t"], t_f1 + 4 * g + 4)
                    for hi in range(4):
                        hc = 4 * g + hi
                        ii = v.tensor_tensor(out=fbk(g, hc)[:],
                                             in0=fbk(g, hc)[:],
                                             in1=qc2[:], op=ALU.mult)
                    einc("v", ii)
                    v.drain()
                scl_mark[g] = cnt["v"]

            emit_scale(0)
            emit_scale(1)
            for g in range(8):
                @blk.scalar
                def _(sc, g=g):
                    if g == 0 and s == 0:
                        sc.wait_ge(sems["v"], pm["v_cwv"])
                    sc.wait_ge(sems["v"], scl_mark[g])
                    if g >= 2:
                        sc.wait_ge(sems["v"], cg_mark[g - 2])
                    for hi in range(4):
                        hc = 4 * g + hi
                        st = stx[hc % 8]
                        sc.copy(out=st[:, 0:2], in_=tails[:, hc, :])
                        sc.copy(out=st[:, 2:514], in_=fbk(g, hc)[:])
                    sc.drain()
                    for hi in range(4):
                        hc = 4 * g + hi
                        ii = sc.copy(out=tails[:, hc, :],
                                     in_=fbk(g, hc)[:, 510:512])
                    einc("s", ii)
                    if g >= 1:
                        gq = g - 1
                        sc.wait_ge(sems["v"], cg_mark[gq])
                        for hi in range(4):
                            hc = 4 * gq + hi
                            ii = sc.activation(out=silu_ap(hc, n),
                                               in_=tg(gq)[hc % 4][0:P, 0:n],
                                               func=AF.Silu,
                                               bias=cbv[:, hc:hc + 1])
                        einc("s", ii)
                    sc.drain()

                @blk.vector
                def _(v, g=g):
                    if g == 0:
                        v.wait_ge(sems["sout"], sout_prev)
                    v.wait_ge(sems["s"], egm[g])
                    for hi in range(4):
                        hc = 4 * g + hi
                        st = stx[hc % 8]
                        v.tensor_scalar(out=tg(g)[hc % 4][0:P, 0:n],
                                        in0=st[:, j0:j0 + n],
                                        scalar1=cwv[:, 3 * hc:3 * hc + 1],
                                        scalar2=None, op0=ALU.mult)
                    v.drain()
                    for hi in range(4):
                        hc = 4 * g + hi
                        st = stx[hc % 8]
                        v.scalar_tensor_tensor(
                            out=tbv[hc % 4][0:P, 0:n],
                            in0=st[:, j0 + 1:j0 + 1 + n],
                            scalar=cwv[:, 3 * hc + 1:3 * hc + 2],
                            in1=tg(g)[hc % 4][0:P, 0:n],
                            op0=ALU.mult, op1=ALU.add)
                    v.drain()
                    for hi in range(4):
                        hc = 4 * g + hi
                        st = stx[hc % 8]
                        ii = v.scalar_tensor_tensor(
                            out=tg(g)[hc % 4][0:P, 0:n],
                            in0=st[:, j0 + 2:j0 + 2 + n],
                            scalar=cwv[:, 3 * hc + 2:3 * hc + 3],
                            in1=tbv[hc % 4][0:P, 0:n],
                            op0=ALU.mult, op1=ALU.add)
                    einc("v", ii)
                    v.drain()
                cg_mark[g] = cnt["v"]
                if g + 2 <= 7:
                    emit_scale(g + 2)
                if fill is not None:
                    fill(1)

            @blk.scalar
            def _(sc):
                gq = 7
                sc.wait_ge(sems["v"], cg_mark[gq])
                sc.drain()
                for hi in range(4):
                    hc = 4 * gq + hi
                    ii = sc.activation(out=silu_ap(hc, n),
                                       in_=tg(gq)[hc % 4][0:P, 0:n],
                                       func=AF.Silu,
                                       bias=cbv[:, hc:hc + 1])
                einc("s", ii)
            s_si_all = cnt["s"]
            assert s_si_all == sgm[7]
            v_cv_all = cnt["v"]

            if s == 0:
                @blk.sync
                def _(sy):
                    sy.wait_ge(sems["v"], v_cv_all)
                    sy.wait_ge(sems["s"], sgm[7])
                    dinc("ldx1", sy.dma_start(
                        out=fl512(X[1]),
                        in_=xin[:, :, lo + TS:lo + 2 * TS]))
            emit_ldf2()

            if last:
                cw3 = cwv[:].rearrange("p (a b) -> p a b", b=3)

                @blk.vector
                def _(v):
                    v.wait_ge(sems["s"], s_si_all)
                    v.tensor_tensor(out=stail[:], in0=tails[:, :, 0],
                                    in1=cw3[:, :, 0], op=ALU.mult)
                    v.drain()
                    v.tensor_tensor(out=hmt[0][:, 0:HC], in0=tails[:, :, 1],
                                    in1=cw3[:, :, 1], op=ALU.mult)
                    v.drain()
                    v.tensor_tensor(out=stail[:], in0=stail[:],
                                    in1=hmt[0][:, 0:HC], op=ALU.add)
                    v.drain()
                    einc("v", v.tensor_tensor(out=stail[:], in0=stail[:],
                                              in1=cbv[:], op=ALU.add))
                    v.drain()
                v_tc = cnt["v"]

                @blk.scalar
                def _(sc):
                    sc.wait_ge(sems["v"], v_tc)
                    einc("s", sc.activation(out=stail[:], in_=stail[:],
                                            func=AF.Silu))
                    sc.drain()
                s_ts = cnt["s"]

            @blk.vector
            def _(v):
                for u in range(4):
                    v.wait_ge(sems["s"], sgm[2 * u + 1])
                    v.scalar_tensor_tensor(
                        out=Y0[:, :, 0:n],
                        in0=SILU[u][:, :, 0:n], scalar=-1.0,
                        in1=SILU[u][:, :, 0:n],
                        op0=ALU.mult, op1=ALU.max)
                    v.drain()
                    v.tensor_tensor(out=Y0[:, 0:4, 0:n], in0=Y0[:, 0:4, 0:n],
                                    in1=Y0[:, 4:8, 0:n], op=ALU.max)
                    v.drain()
                    v.tensor_tensor(out=Y0[:, 0:2, 0:n], in0=Y0[:, 0:2, 0:n],
                                    in1=Y0[:, 2:4, 0:n], op=ALU.max)
                    v.drain()
                    if u == 0:
                        if n < TS:
                            v.memset(rmx[:], 0.0)
                            v.drain()
                        ii = v.tensor_tensor(out=rmx[:, 0:n],
                                             in0=Y0[:, 0, 0:n],
                                             in1=Y0[:, 1, 0:n], op=ALU.max)
                    else:
                        v.tensor_tensor(out=Y0[:, 0, 0:n], in0=Y0[:, 0, 0:n],
                                        in1=Y0[:, 1, 0:n], op=ALU.max)
                        v.drain()
                        ii = v.tensor_tensor(out=rmx[:, 0:n], in0=rmx[:, 0:n],
                                             in1=Y0[:, 0, 0:n], op=ALU.max)
                    einc("v", ii)
                    v.drain()
            v_rm2 = cnt["v"]

            t_tp = cnt["t"]
            v_tp_b = cnt["v"]
            tp2bank = (pb[3], pb[4], pb[5], pb[2])

            @blk.tensor
            def _(t):
                t.wait_ge(sems["v"], v_rm2)
                for j in range(4):
                    if j >= 1:
                        t.wait_ge(sems["v"], v_tp_b + j)
                        einc("t", t.transpose(
                            tp2bank[j - 1][0:1, (j - 1) * P:j * P],
                            smx[:, j - 1:j], idf[:]))
                    einc("t", t.transpose(pb[2][:, 0:P],
                                          rmx[:, j * P:(j + 1) * P], idf[:]))
                t.wait_ge(sems["v"], v_tp_b + 4)
                einc("t", t.transpose(tp2bank[3][0:1, 3 * P:4 * P],
                                      smx[:, 3:4], idf[:]))

            @blk.vector
            def _(v):
                for j in range(4):
                    v.wait_ge(sems["t"], t_tp + 2 * j + 1)
                    ii = v.reduce_max(out=smx[:, j:j + 1], in_=pb[2][:, 0:P],
                                      axis=AX.X)
                    einc("v", ii)
            v_sm2 = cnt["v"]
            t_tp8 = cnt["t"]

            @blk.scalar
            def _(sc):
                sc.wait_ge(sems["t"], t_tp8)
                for j in range(4):
                    ii = sc.copy(
                        out=wqsf[0:1, 3 * TS + j * P:3 * TS + (j + 1) * P],
                        in_=tp2bank[j][0:1, j * P:(j + 1) * P])
                einc("s", ii)
                sc.drain()
            s_mr2 = cnt["s"]

            t_mxr2 = cnt["t"]

            @blk.tensor
            def _(t):
                t.wait_ge(sems["s"], s_mr2)
                einc("t", t.matmul(pb[2][:], ones_rf[:],
                                   wqsf[0:1, 3 * TS:4 * TS],
                                   start=True, stop=True))

            @blk.scalar
            def _(sc):
                sc.wait_ge(sems["t"], t_mxr2 + 1)
                einc("s", sc.copy(out=reps[2][:], in_=pb[2][:]))
                sc.drain()
            s_mx2 = cnt["s"]

            @blk.vector
            def _(v):
                v.wait_ge(sems["s"], s_mx2)
                v.tensor_scalar(out=reps[2][:], in0=reps[2][:], scalar1=1e-5,
                                scalar2=None, op0=ALU.max)
                v.drain()
                v.tensor_scalar(out=reps[3][:], in0=reps[2][:],
                                scalar1=1.0 / 127, scalar2=None, op0=ALU.mult)
                v.drain()
                v.reciprocal(out=reps[2][:], in_=reps[2][:])
                v.drain()
                einc("v", v.tensor_scalar(out=repb[:], in0=reps[2][:],
                                          scalar1=127.0, scalar2=None,
                                          op0=ALU.mult))
                v.drain()
            v_s2 = cnt["v"]

            @blk.vector
            def _(v):
                v.wait_ge(sems["v"], v_s2)
                NP = HC // 2
                hmv = [wqsf[:, 4 * TS:6 * TS].rearrange(
                           "p (a b) -> p a b", b=TS),
                       flat(X[s % 2])[:, 0:2 * TS].rearrange(
                           "p (a b) -> p a b", b=TS)]
                qiv = [qiw, qiw2]
                qcv = [qcw, qcw2]
                c2v = [flat(X[s % 2])[:, 2 * TS:4 * TS].rearrange(
                           "p (a b) -> p a b", b=TS),
                       flat(X[s % 2])[:, 4 * TS:6 * TS].rearrange(
                           "p (a b) -> p a b", b=TS)]
                rb = repb[:, None, 0:n].broadcast_to([P, 2, n])

                def ap(k):
                    u, cch = (2 * k) // 8, (2 * k) % 8
                    p_ = k % 2
                    return (SILU[u][:, cch:cch + 2, 0:n],
                            hmv[p_][:, :, 0:n], qiv[p_][:, :, 0:n],
                            qcv[p_][:, :, 0:n], c2v[p_][:, :, 0:n])

                def s_m(k):
                    sp, hm, qi, qc, c2 = ap(k)
                    v.tensor_tensor(out=hm, in0=sp, in1=rb, op=ALU.mult)

                def s_i16(k):
                    sp, hm, qi, qc, c2 = ap(k)
                    v.tensor_scalar(out=qi, in0=hm, scalar1=-0.5,
                                    scalar2=None, op0=ALU.add)

                def s_c12(k):
                    sp, hm, qi, qc, c2 = ap(k)
                    v.scalar_tensor_tensor(out=qc, in0=hm, scalar=-0.5,
                                           in1=qi, op0=ALU.add, op1=ALU.is_gt)
                    v.scalar_tensor_tensor(out=c2, in0=hm, scalar=-1.5,
                                           in1=qi, op0=ALU.add, op1=ALU.is_gt)

                s_m(0)
                v.drain()
                s_i16(0)
                v.drain()
                s_c12(0)
                v.drain()
                for k in range(NP):
                    sp, hm, qi, qc, c2 = ap(k)
                    v.tensor_tensor(out=sp, in0=qi, in1=qc, op=ALU.add)
                    if k + 1 < NP:
                        s_m(k + 1)
                    v.drain()
                    v.scalar_tensor_tensor(out=qc, in0=hm, scalar=0.5,
                                           in1=qi, op0=ALU.add, op1=ALU.is_lt)
                    v.tensor_tensor(out=sp, in0=sp, in1=c2, op=ALU.add)
                    if k + 1 < NP:
                        s_i16(k + 1)
                    v.drain()
                    ii = v.tensor_tensor(out=sp, in0=sp, in1=qc,
                                         op=ALU.subtract)
                    einc("v", ii)
                    if k + 1 < NP:
                        s_c12(k + 1)
                    v.drain()
            v_xq2 = cnt["v"]

            fr_next = None
            if front_cb is not None:
                fr_next = front_cb({"t_f1e": t_f1e, "t_mxr2": t_mxr2 + 1,
                                    "v_s2": v_s2, "egm7": egm[7]})

            if last:
                @blk.vector
                def _(v):
                    v.wait_ge(sems["s"], s_ts)
                    v.wait_ge(sems["v"], v_xq2)
                    einc("v", v.reduce_max(out=smx[:, 0:1], in_=stail[:],
                                           axis=AX.X,
                                           apply_absolute_value=True))
                    v.drain()
                v_tm = cnt["v"]

                t_ttp = cnt["t"]

                @blk.tensor
                def _(t):
                    t.wait_ge(sems["v"], v_tm)
                    einc("t", t.transpose(pb[3][0:1, 0:P], smx[:, 0:1],
                                          idf[:]))

                @blk.vector
                def _(v):
                    v.wait_ge(sems["t"], t_ttp + 1)
                    v.reduce_max(out=mrow[0:1, 0:1], in_=pb[3][0:1, 0:P],
                                 axis=AX.X)
                    v.drain()
                    v.tensor_scalar(out=mrow[0:1, 0:1], in0=mrow[0:1, 0:1],
                                    scalar1=1e-5, scalar2=None, op0=ALU.max)
                    v.drain()
                    v.tensor_scalar(out=mrow[0:1, 1:2], in0=mrow[0:1, 0:1],
                                    scalar1=1.0 / 127, scalar2=None,
                                    op0=ALU.mult)
                    v.drain()
                    v.reciprocal(out=mrow[0:1, 2:3], in_=mrow[0:1, 0:1])
                    v.drain()
                    einc("v", v.tensor_scalar(out=mrow[0:1, 2:3],
                                              in0=mrow[0:1, 2:3],
                                              scalar1=127.0, scalar2=None,
                                              op0=ALU.mult))
                    v.drain()
                v_ts2 = cnt["v"]

                t_trp = cnt["t"]

                @blk.tensor
                def _(t):
                    t.wait_ge(sems["v"], v_ts2)
                    einc("t", t.matmul(pb[3][:, 0:2], ones_rf[:],
                                       mrow[0:1, 1:3], start=True, stop=True))

                @blk.scalar
                def _(sc):
                    sc.wait_ge(sems["t"], t_trp + 1)
                    einc("s", sc.copy(out=smx[:, 2:4], in_=pb[3][:, 0:2]))
                    sc.drain()
                s_trp = cnt["s"]

                @blk.vector
                def _(v):
                    v.wait_ge(sems["s"], s_trp)
                    v.tensor_scalar(out=hmt[0][:, 0:HC], in0=stail[:],
                                    scalar1=smx[:, 3:4],
                                    scalar2=None, op0=ALU.mult)
                    v.drain()
                    v.tensor_scalar(out=qia[:, 0:HC], in0=hmt[0][:, 0:HC],
                                    scalar1=-0.5, scalar2=None, op0=ALU.add)
                    v.drain()
                    v.scalar_tensor_tensor(out=qcb[:, 0:HC],
                                           in0=hmt[0][:, 0:HC], scalar=-0.5,
                                           in1=qia[:, 0:HC],
                                           op0=ALU.add, op1=ALU.is_gt)
                    v.scalar_tensor_tensor(out=qzb[:, 0:HC],
                                           in0=hmt[0][:, 0:HC], scalar=-1.5,
                                           in1=qia[:, 0:HC],
                                           op0=ALU.add, op1=ALU.is_gt)
                    v.scalar_tensor_tensor(out=rmx[:, 0:HC],
                                           in0=hmt[0][:, 0:HC], scalar=0.5,
                                           in1=qia[:, 0:HC],
                                           op0=ALU.add, op1=ALU.is_lt)
                    v.drain()
                    v.tensor_tensor(out=qtail[:], in0=qia[:, 0:HC],
                                    in1=qcb[:, 0:HC], op=ALU.add)
                    v.drain()
                    v.tensor_tensor(out=qtail[:], in0=qtail[:],
                                    in1=qzb[:, 0:HC], op=ALU.add)
                    v.drain()
                    einc("v", v.tensor_tensor(out=qtail[:], in0=qtail[:],
                                              in1=rmx[:, 0:HC],
                                              op=ALU.subtract))
                    v.drain()
                v_qt = cnt["v"]

            t_f2 = cnt["t"]
            v_f2_b = cnt["v"]
            ninc_t = 2 if last else 1
            ninc_v = 2 if last else 1

            @blk.tensor
            def _(t):
                for dc in range(CC):
                    t.wait_ge(sems["ldf2%d" % (dc % 2)],
                              ldf_b["2%d" % (dc % 2)] + 16 * (dc // 2 + 1))
                    t.wait_ge(sems["v"], v_xq2)
                    if last:
                        t.wait_ge(sems["v"], v_qt)
                    if dc >= 2:
                        t.wait_ge(sems["v"], v_f2_b + ninc_v * (dc - 2) + ninc_v)
                    for hc in range(HC):
                        ii = t.matmul(pb[dc % 2][:, 0:n],
                                      w2view(W2U[dc % 2])[:, hc, :],
                                      silu_ap(hc, n),
                                      start=(hc == 0), stop=(hc == HC - 1))
                    einc("t", ii)
                    if last:
                        for hc in range(HC):
                            ii = t.matmul(pb[2 + dc % 2][:, 0:1],
                                          w2view(W2U[dc % 2])[:, hc, :],
                                          qtail[:, hc:hc + 1],
                                          start=(hc == 0), stop=(hc == HC - 1))
                        einc("t", ii)
                    t.wait_ge(sems["t"], cnt["t"])
                    t.nop().then_inc(sems["wff"], 1)
                    cnt["wff"] += 1
            t_f2e = cnt["t"]

            @blk.vector
            def _(v):
                v.wait_ge(sems["sout"], sout_prev)
                for dc in range(CC):
                    v.wait_ge(sems["t"], t_f2 + ninc_t * dc + 1)
                    v.wait_ge(sems["x2w"], x2w_e)
                    v.scalar_tensor_tensor(out=OUTS[:, dc, 0:n],
                                           in0=pb[dc % 2][:, 0:n],
                                           scalar=m2v[:], in1=reps[3][:, 0:n],
                                           op0=ALU.mult, op1=ALU.mult)
                    v.drain()
                    ii = v.tensor_tensor(out=OUTS[:, dc, 0:n],
                                         in0=OUTS[:, dc, 0:n],
                                         in1=X2W[:, dc, 0:n], op=ALU.add)
                    einc("v", ii)
                    if last:
                        v.wait_ge(sems["t"], t_f2 + ninc_t * dc + 2)
                        v.tensor_scalar(out=OUTS[:, dc, n:n + 1],
                                        in0=pb[2 + dc % 2][:, 0:1],
                                        scalar1=m2v[:], scalar2=smx[:, 2:3],
                                        op0=ALU.mult, op1=ALU.mult)
                        v.drain()
                        ii = v.tensor_tensor(out=OUTS[:, dc, n:n + 1],
                                             in0=OUTS[:, dc, n:n + 1],
                                             in1=ttmp[:, dc:dc + 1],
                                             op=ALU.add)
                        einc("v", ii)
                        v.drain()
            v_outs = cnt["v"]

            @blk.scalar
            def _(sc):
                sc.wait_ge(sems["v"], v_outs)
                no = n + 1 if last else n
                dinc("sout", sc.dma_start(out=out[:, :, wlo:wlo + no],
                                          in_=OUTS[:, :, 0:no]))

            return ({"t_f1e": t_f1e, "t_f2e": t_f2e, "sout_e": cnt["sout"],
                     "v_cv": v_cv_all, "sgm7": sgm[7]}, fr_next)

    with block() as _mblk:
        def fill(nu):
            for _ in range(nu):
                if prep_work:
                    prep_work.pop(0)(_mblk)

        fr = attn_front(_mblk, 0, None, fill)
        fm = None
        for s in range(NSEG):
            am = attn_rest(_mblk, s, fr, fm, fill if s == 0 else None)
            cb = None
            if s + 1 < NSEG:
                cb = lambda ctx, s1=s + 1: attn_front(_mblk, s1, ctx)
            fm, fr = ffn_seg(_mblk, s, am, fill if s == 0 else None, cb, fm)
            if s == 0:
                assert not prep_work, f"prep leftover {len(prep_work)}"

    with block() as blk:
        total_out = cnt["sout"]

        @blk.sync
        def _(sy):
            sy.wait_ge(sems["sout"], total_out)

    es.close()
    return nc


def _arr(a, nchunk):
    a = np.ascontiguousarray(a, dtype=np.float32)
    return np.ascontiguousarray(a.reshape(nchunk, P, -1).transpose(1, 0, 2))


_NC_CACHE = {}


def kernel(L0_attn_x, L0_attn_W0, L0_attn_A_init, L0_attn_B_init, L0_attn_log_lr,
           L0_attn_Wq, L0_attn_Wk, L0_attn_Wv, L0_attn_Wo, L0_attn_Wg,
           L0_attn_ln_g, L0_attn_ln_b, L0_attn_ls_g, L0_ffn_w1, L0_ffn_conv_w,
           L0_ffn_conv_b, L0_ffn_w2):
    if "nc" not in _NC_CACHE:
        _NC_CACHE["nc"] = build_program()
    nc = _NC_CACHE["nc"]

    vec = lambda a: np.ascontiguousarray(
        np.asarray(a, np.float32).reshape(CC, P).T)
    common = {
        "wqt": _arr(np.asarray(L0_attn_Wq).T, CC),
        "wkt": _arr(np.asarray(L0_attn_Wk).T, CC),
        "wvt": _arr(np.asarray(L0_attn_Wv).T, CC),
        "wgt": _arr(np.asarray(L0_attn_Wg).T, CC),
        "wot": _arr(np.asarray(L0_attn_Wo).T, CC),
        "w0n": _arr(np.asarray(L0_attn_W0), CC),
        "w1t": _arr(np.asarray(L0_ffn_w1).T, CC),
        "w2t": _arr(np.asarray(L0_ffn_w2).T, HC),
        "a_t": np.ascontiguousarray(np.asarray(L0_attn_A_init, np.float32).T),
        "b_i": np.ascontiguousarray(np.asarray(L0_attn_B_init, np.float32)),
        "vlr": vec(L0_attn_log_lr),
        "vg": vec(L0_attn_ln_g),
        "vb": vec(L0_attn_ln_b),
        "vls": vec(L0_attn_ls_g),
        "vcw": np.ascontiguousarray(
            np.asarray(L0_ffn_conv_w, np.float32).reshape(HC, P, 3)
            .transpose(1, 0, 2)),
        "vcb": np.ascontiguousarray(np.asarray(L0_ffn_conv_b, np.float32).reshape(HC, P).T),
    }
    x = np.asarray(L0_attn_x, np.float32)
    B = x.shape[0]
    in_maps = []
    for b in range(B):
        m = dict(common)
        m["xin"] = _arr(x[b].T, CC)
        in_maps.append(m)
    res = run_bass_kernel_spmd(nc, in_maps, core_ids=list(range(B)))
    outs = []
    for b in range(B):
        arr = np.asarray(res.results[b]["out"])
        outs.append(arr.transpose(2, 1, 0).reshape(T, C))
    return np.stack(outs, axis=0).astype(np.float32)
